# version 01
# baseline (speedup 1.0000x reference)
from contextlib import ExitStack

import numpy as np
import ml_dtypes

import concourse.bass as bass
import concourse.mybir as mybir
from concourse.bass_utils import run_bass_kernel_spmd
from concourse.bass import IndirectOffsetOnAxis
from concourse.alu_op_type import AluOpType
from concourse.ordered_engine_set import OrderedEngineSet

F32 = mybir.dt.float32
BF16 = mybir.dt.bfloat16
I32 = mybir.dt.int32
U32 = mybir.dt.uint32
AF = mybir.ActivationFunctionType
BF = ml_dtypes.bfloat16
TT = AluOpType

B, S, D = 4, 4096, 1024
NTOK = B * S
NCORES = 8
TPC = NTOK // NCORES
H, HD = 16, 64
DQ = DKV = 512
DFH = 2730
WIN = 256
NEXP, CAP, DFF, RH = 16, 256, 2048, 128


def _l0_rope_tables():
    pos = np.arange(S, dtype=np.float32)[:, None]
    ids = np.arange(D // 2, dtype=np.float32)
    theta = (np.float32(10000.0) ** (np.float32(-2.0) * ids)) / np.float32(D // 2)
    r = (pos * theta[None, :]).astype(np.float32)
    return np.cos(r).astype(np.float32), np.sin(r).astype(np.float32)


_PERM_EO = np.concatenate([np.arange(0, D, 2), np.arange(1, D, 2)])


def bfc(x):
    return np.ascontiguousarray(np.asarray(x)).astype(BF)


def f32c(x):
    return np.ascontiguousarray(np.asarray(x), dtype=np.float32)



class Ctr:
    def __init__(self, es, nc, name):
        self.sem = es.enter_context(nc.semaphore(name))
        self.v = 0

    def inc(self, instr, n=1):
        instr.then_inc(self.sem, n)
        self.v += n
        return instr

    def wait(self, eng):
        eng.wait_ge(self.sem, self.v)


class LP:

    def __init__(self, name):
        self.nc = bass.Bass("TRN2", target_bir_lowering=False)
        self.es = ExitStack()
        self.blk = None
        self.name = name

    def inp(self, name, shape, dtype):
        return self.nc.dram_tensor(name, list(shape), dtype, kind="ExternalInput").ap()

    def out(self, name, shape, dtype):
        return self.nc.dram_tensor(name, list(shape), dtype, kind="ExternalOutput").ap()

    def sb(self, name, shape, dtype):
        return self.es.enter_context(self.nc.sbuf_tensor(name, list(shape), dtype))

    def ps(self, name, shape, dtype=F32):
        return self.es.enter_context(self.nc.psum_tensor(name, list(shape), dtype))

    def ctr(self, name):
        return Ctr(self.es, self.nc, name)

    def begin(self):
        self.blk = self.es.enter_context(self.nc.Block())
        return self.blk

    def done(self):
        self.es.close()
        return self.nc


def _load_weight_chunks(g, ldw, wsb, wdram, kdim, odim):
    for c in range(kdim // 128):
        ldw.inc(g.dma_start(out=wsb[:, c * odim:(c + 1) * odim],
                            in_=wdram[c * 128:(c + 1) * 128, :]), 16)



def build_b0_qkv():
    L = LP("b0qkv")
    nc = L.nc
    x = L.inp("x", [TPC, D], F32)
    cosd = L.inp("cos", [TPC, D // 2], F32)
    sind = L.inp("sin", [TPC, D // 2], F32)
    eye = L.inp("eye", [128, 128], BF16)
    wshape = {"wdq": (D, DQ), "wuq": (DQ, D), "wqr": (DQ, D), "wkr": (D, D),
              "wdkv": (D, DKV), "wuk": (DKV, D), "wuv": (DKV, D)}
    wd = {n: L.inp(n, sh, F32) for n, sh in wshape.items()}
    bdq_f = L.inp("bdq_f", [128, 4], F32)
    bdkv_f = L.inp("bdkv_f", [128, 4], F32)
    brow = {n: L.inp("b_" + n, [1, D], F32) for n in ["wuq", "wqr", "wkr", "wuk", "wuv"]}
    q_o = L.out("q", [TPC, D], BF16)
    k_o = L.out("k", [TPC, D], BF16)
    v_o = L.out("v", [TPC, D], BF16)

    NT = TPC // 128
    wsb = {n: L.sb("s_" + n, [128, (sh[0] // 128) * sh[1]], BF16) for n, sh in wshape.items()}
    bbc = {n: L.sb("bb_" + n, [128, D], F32) for n in brow}
    bdqs = L.sb("bdqs", [128, 4], F32)
    bdkvs = L.sb("bdkvs", [128, 4], F32)
    eyesb = L.sb("eyesb", [128, 128], BF16)
    xt = L.sb("xt", [128, D], F32)
    xnb = L.sb("xnb", [128, D], BF16)
    stats = L.sb("stats", [128, 12], F32)
    aggr = L.sb("aggr", [128, 2], F32)
    lnt = L.sb("lnt", [128, 2], F32)
    epsc = L.sb("epsc", [128, 1], F32)
    xnfm = L.sb("xnfm", [128, 8 * 128], BF16)
    ctq = L.sb("ctq", [128, 4 * 128], BF16)
    ctkv = L.sb("ctkv", [128, 4 * 128], BF16)
    yr = L.sb("yr", [128, D], F32)
    rp = L.sb("rp", [128, D], F32)
    tmp = L.sb("tmp", [128, D], F32)
    ycf = L.sb("ycf", [128, D], F32)
    costt = L.sb("costt", [128, D // 2], F32)
    sintt = L.sb("sintt", [128, D // 2], F32)
    outq = L.sb("outq", [128, D], BF16)
    outk = L.sb("outk", [128, D], BF16)
    outv = L.sb("outv", [128, D], BF16)

    pfm = L.ps("pfm", [128, 128], BF16)
    pcc = L.ps("pcc", [128, 128], F32)
    pya = L.ps("pya", [128, 1024], F32)
    pyb = L.ps("pyb", [128, 1024], F32)

    ldw = L.ctr("ldw")
    ldx = L.ctr("ldx")
    vv = L.ctr("vv")
    ss = L.ctr("ss")
    mm = L.ctr("mm")
    st = L.ctr("st")

    blk = L.begin()

    def on(engf, f):
        engf(f)

    def g_loads(g):
        for n, sh in wshape.items():
            _load_weight_chunks(g, ldw, wsb[n], wd[n], sh[0], sh[1])
        ldw.inc(g.dma_start(out=bdqs[:], in_=bdq_f[:]), 16)
        ldw.inc(g.dma_start(out=bdkvs[:], in_=bdkv_f[:]), 16)
        ldw.inc(g.dma_start(out=eyesb[:], in_=eye[:]), 16)
        for n in brow:
            ldw.inc(g.dma_start(out=bbc[n][:], in_=brow[n].to_broadcast([128, D])), 16)
    on(blk.gpsimd, g_loads)

    on(blk.vector, lambda v: vv.inc(v.memset(epsc[:], 1e-5)))

    for t in range(NT):
        r0 = t * 128

        def s_load(s, r0=r0):
            s.dma_start(out=xt[:], in_=x[r0:r0 + 128, :]).then_inc(ldx.sem, 16)
            s.dma_start(out=costt[:], in_=cosd[r0:r0 + 128, :]).then_inc(ldx.sem, 16)
            s.dma_start(out=sintt[:], in_=sind[r0:r0 + 128, :]).then_inc(ldx.sem, 16)
            ldx.v += 48
        on(blk.sync, s_load)

        def v_ln1(v):
            v.wait_ge(ldx.sem, ldx.v)
            vv.inc(v.bn_stats(out=stats[:, 0:6], in_=xt[:, 0:512]))
            vv.inc(v.bn_stats(out=stats[:, 6:12], in_=xt[:, 512:1024]))
            vv.wait(v)
            vv.inc(v.bn_aggr(out=aggr[:], in_=stats[:]))
        on(blk.vector, v_ln1)

        def s_ln2(sc, t=t):
            if t == 0:
                ldw.wait(sc)
            vv.wait(sc)
            ss.inc(sc.activation(out=lnt[:, 0:1], in_=aggr[:, 1:2], func=AF.Ln,
                                 bias=epsc[:, 0:1]))
            ss.wait(sc)
            ss.inc(sc.activation(out=lnt[:, 1:2], in_=lnt[:, 0:1], func=AF.Exp,
                                 scale=-0.5))
        on(blk.scalar, s_ln2)

        def v_ln3(v):
            ss.wait(v)
            vv.inc(v.tensor_scalar(out=xnb[:], in0=xt[:], scalar1=aggr[:, 0:1],
                                   scalar2=lnt[:, 1:2], op0=TT.subtract, op1=TT.mult))
        on(blk.vector, v_ln3)

        for c in range(8):
            def t_tr(te, c=c):
                if c == 0:
                    vv.wait(te)
                    ldw.wait(te)
                else:
                    ss.wait(te)
                mm.inc(te.transpose(pfm[:], xnb[:, c * 128:(c + 1) * 128], eyesb[:]))
            on(blk.tensor, t_tr)

            def s_ev(sc, c=c):
                mm.wait(sc)
                ss.inc(sc.copy(out=xnfm[:, c * 128:(c + 1) * 128], in_=pfm[:]))
            on(blk.scalar, s_ev)

        for nm, dst, bsrc in (("wdq", ctq, bdqs), ("wdkv", ctkv, bdkvs)):
            for oc in range(4):
                def t_mm(te, nm=nm, oc=oc):
                    ss.wait(te)
                    for c in range(8):
                        ins = te.matmul(pcc[:], wsb[nm][:, c * DQ + oc * 128:c * DQ + oc * 128 + 128],
                                        xnfm[:, c * 128:(c + 1) * 128],
                                        start=(c == 0), stop=(c == 7))
                        if c == 7:
                            mm.inc(ins)
                on(blk.tensor, t_mm)

                def s_ev(sc, dst=dst, bsrc=bsrc, oc=oc):
                    mm.wait(sc)
                    ss.inc(sc.activation(out=dst[:, oc * 128:(oc + 1) * 128], in_=pcc[:],
                                         func=AF.Identity, bias=bsrc[:, oc:oc + 1]))
                on(blk.scalar, s_ev)

        def tm_proj(te, wname, src, kch, psumt, wait_ctr):
            wait_ctr.wait(te)
            for half in range(2):
                for c in range(kch):
                    ins = te.matmul(psumt[:, half * 512:(half + 1) * 512],
                                    src[:, c * 128:(c + 1) * 128],
                                    wsb[wname][:, c * D + half * 512:c * D + half * 512 + 512],
                                    start=(c == 0), stop=(c == kch - 1))
                    if half == 1 and c == kch - 1:
                        mm.inc(ins)


        def rope_emit(v):
            lo, hi = slice(0, 512), slice(512, 1024)
            vv.inc(v.tensor_tensor(out=tmp[:, lo], in0=yr[:, lo], in1=costt[:], op=TT.mult))
            vv.inc(v.tensor_tensor(out=tmp[:, hi], in0=yr[:, hi], in1=sintt[:], op=TT.mult))
            vv.wait(v)
            vv.inc(v.tensor_tensor(out=rp[:, lo], in0=tmp[:, lo], in1=tmp[:, hi], op=TT.subtract))
            vv.inc(v.tensor_tensor(out=tmp[:, lo], in0=yr[:, hi], in1=costt[:], op=TT.mult))
            vv.inc(v.tensor_tensor(out=tmp[:, hi], in0=yr[:, lo], in1=sintt[:], op=TT.mult))
            vv.wait(v)
            vv.inc(v.tensor_tensor(out=rp[:, hi], in0=tmp[:, lo], in1=tmp[:, hi], op=TT.add))

        on(blk.tensor, lambda te: tm_proj(te, "wqr", ctq, 4, pya, ss))

        def v_yrq(v, t=t):
            if t == 0:
                ldw.wait(v)
            mm.wait(v)
            vv.inc(v.tensor_tensor(out=yr[:], in0=pya[:], in1=bbc["wqr"][:], op=TT.add))
            vv.wait(v)
            rope_emit(v)
        on(blk.vector, v_yrq)

        on(blk.tensor, lambda te: tm_proj(te, "wuq", ctq, 4, pyb, vv))

        def v_ycq(v):
            mm.wait(v)
            vv.inc(v.tensor_tensor(out=ycf[:], in0=pyb[:], in1=bbc["wuq"][:], op=TT.add))
            vv.wait(v)
            vv.inc(v.tensor_tensor(out=outq[:], in0=ycf[:], in1=rp[:], op=TT.add))
        on(blk.vector, v_ycq)

        def s_stq(s):
            vv.wait(s)
            st.inc(s.dma_start(out=q_o[r0:r0 + 128, :], in_=outq[:]), 16)
        on(blk.sync, s_stq)

        on(blk.tensor, lambda te: tm_proj(te, "wkr", xnfm, 8, pya, vv))

        def v_yrk(v):
            mm.wait(v)
            vv.inc(v.tensor_tensor(out=yr[:], in0=pya[:], in1=bbc["wkr"][:], op=TT.add))
            vv.wait(v)
            rope_emit(v)
        on(blk.vector, v_yrk)

        on(blk.tensor, lambda te: tm_proj(te, "wuk", ctkv, 4, pyb, vv))

        def v_yck(v):
            mm.wait(v)
            vv.inc(v.tensor_tensor(out=ycf[:], in0=pyb[:], in1=bbc["wuk"][:], op=TT.add))
            vv.wait(v)
            vv.inc(v.tensor_tensor(out=outk[:], in0=ycf[:], in1=rp[:], op=TT.add))
        on(blk.vector, v_yck)

        def s_stk(s):
            vv.wait(s)
            st.inc(s.dma_start(out=k_o[r0:r0 + 128, :], in_=outk[:]), 16)
        on(blk.sync, s_stk)

        on(blk.tensor, lambda te: tm_proj(te, "wuv", ctkv, 4, pya, vv))

        def v_v(v):
            mm.wait(v)
            vv.inc(v.tensor_tensor(out=outv[:], in0=pya[:], in1=bbc["wuv"][:], op=TT.add))
        on(blk.vector, v_v)

        def s_stv(s):
            vv.wait(s)
            st.inc(s.dma_start(out=v_o[r0:r0 + 128, :], in_=outv[:]), 16)
            st.wait(s)
        on(blk.sync, s_stv)

    return L.done()



_EYE = np.eye(128).astype(BF)


def _run(nc, in_maps):
    res = run_bass_kernel_spmd(nc, in_maps, core_ids=list(range(NCORES)))
    return res.results


def l01_inmaps(x_flat, W):
    cos, sin = _l0_rope_tables()
    cosf = np.tile(cos, (B, 1))
    sinf = np.tile(sin, (B, 1))
    perm = _PERM_EO
    base = {
        "eye": _EYE,
        "wdq": f32c(W["L0_attn_Wd_q_w"]),
        "wuq": f32c(W["L0_attn_Wu_q_w"][:, perm]),
        "wqr": f32c(W["L0_attn_Wq_r_w"][:, perm]),
        "wkr": f32c(W["L0_attn_Wk_r_w"][:, perm]),
        "wdkv": f32c(W["L0_attn_Wd_kv_w"]),
        "wuk": f32c(W["L0_attn_Wu_k_w"][:, perm]),
        "wuv": f32c(W["L0_attn_Wu_v_w"]),
        "bdq_f": f32c(W["L0_attn_Wd_q_b"]).reshape(4, 128).T.copy(),
        "bdkv_f": f32c(W["L0_attn_Wd_kv_b"]).reshape(4, 128).T.copy(),
        "b_wuq": f32c(W["L0_attn_Wu_q_b"][perm]).reshape(1, D),
        "b_wqr": f32c(W["L0_attn_Wq_r_b"][perm]).reshape(1, D),
        "b_wkr": f32c(W["L0_attn_Wk_r_b"][perm]).reshape(1, D),
        "b_wuk": f32c(W["L0_attn_Wu_k_b"][perm]).reshape(1, D),
        "b_wuv": f32c(W["L0_attn_Wu_v_b"]).reshape(1, D),
    }
    ims = []
    for r in range(NCORES):
        m = dict(base)
        m["x"] = f32c(x_flat[r * TPC:(r + 1) * TPC])
        m["cos"] = f32c(cosf[r * TPC:(r + 1) * TPC])
        m["sin"] = f32c(sinf[r * TPC:(r + 1) * TPC])
        ims.append(m)
    return ims


_PROGS = {}


def _prog(name, fn):
    if name not in _PROGS:
        _PROGS[name] = fn()
    return _PROGS[name]


def kernel(**inputs):
    W = {k: np.asarray(v) for k, v in inputs.items()}
    x = f32c(W["L0_attn_x"]).reshape(-1, D)

    res = _run(_prog("b0qkv", build_b0_qkv), l01_inmaps(x, W))
    qf = np.concatenate([r["q"] for r in res], 0)
    kf = np.concatenate([r["k"] for r in res], 0)
    vf = np.concatenate([r["v"] for r in res], 0)
    nc2 = _prog("attncore", lambda: build_attn_core(
        nunits=4, scale=float(1.0 / np.sqrt(np.float32(1040.0)))))
    res = _run(nc2, l02_inmaps(qf, kf, vf))
    res = _run(_prog("b0out", build_b0_out), l03_inmaps([r["o"] for r in res], W, x))
    x1 = np.concatenate([r["x1"] for r in res], 0)
    y = np.concatenate([r["y0"] for r in res], 0)

    nca = _prog("innera", build_inner_attn)
    ncf = _prog("innerf", lambda: build_inner_ffn(extra=False))
    ncf2 = _prog("innerf2", lambda: build_inner_ffn(extra=True))
    for layer in range(6):
        res = _run(nca, l04_inmaps(y, W["L0_ffn_Wqkv"][layer],
                                   W["L0_ffn_q_scale"][layer],
                                   W["L0_ffn_k_scale"][layer],
                                   W["L0_ffn_Wo"][layer]))
        y = np.concatenate([r["yo"] for r in res], 0)
        last = layer == 5
        res = _run(ncf2 if last else ncf,
                   l05_inmaps(y, W["L0_ffn_ln_g"][layer], W["L0_ffn_ln_b"][layer],
                              W["L0_ffn_W1"][layer], W["L0_ffn_b1"][layer],
                              W["L0_ffn_W2"][layer], W["L0_ffn_b2"][layer],
                              x_extra=x1 if last else None))
        y = np.concatenate([r["yo"] for r in res], 0)
    x2 = y

    res = _run(_prog("b1qkv", build_b1_qkv), l17_inmaps(x2, W))
    qf = np.concatenate([r["q"] for r in res], 0)
    kf = np.concatenate([r["k"] for r in res], 0)
    vf = np.concatenate([r["v"] for r in res], 0)
    WT = build_wt_host(W["L1_attn_rel_table"])
    res = _run(_prog("b1wv", build_b1_wv), l18_inmaps(WT, vf))
    WVfull = np.zeros((B, 2048, D), np.float32)
    for r in range(NCORES):
        WVfull[r // 2, (r % 2) * 1024:(r % 2 + 1) * 1024] = res[r]["wv"]
    res = _run(_prog("b1attn", build_b1_attn), l19_inmaps(qf, kf, vf, WVfull, x2, W))
    x3 = np.concatenate([r["x3"] for r in res], 0)
    xn3 = np.concatenate([r["xn3"] for r in res], 0)
    lg = np.concatenate([r["lg"] for r in res], 0)

    res = _run(_prog("topk", build_moe_topk), l21_inmaps(lg))
    idx_res = [r["idx"] for r in res]
    wts_res = [r["wts"] for r in res]
    res = _run(_prog("moe", build_moe_expert), l22_inmaps(xn3, idx_res, wts_res, W))
    partials = [r["partial"] for r in res]
    res = _run(_prog("comb", build_combine), l23_inmaps(x3, partials))
    out = np.concatenate([r["out"] for r in res], 0)
    return out.reshape(B, S, D).astype(np.float32)



def build_attn_core(nunits=4, scale=None):
    L = LP("attncore")
    nc = L.nc
    qd = L.inp("q", [nunits, 128, S], BF16)
    kd = L.inp("k", [nunits, 128, S], BF16)
    vd = L.inp("v", [nunits, S, 130], BF16)
    ones = L.inp("ones1", [1, 64], BF16)
    od = L.out("o", [nunits, 128, S], BF16)

    qs = L.sb("qs", [128, S], BF16)
    ks = L.sb("ks", [128, S], BF16)
    vs = L.sb("vs", [128, (S // 128) * 130], BF16)
    onesb = L.sb("onesb", [1, 64], BF16)
    pta = L.sb("pta", [128, 512], BF16)
    ptb = L.sb("ptb", [128, 512], BF16)
    rc = L.sb("rc", [1, 8], F32)
    rchi = L.sb("rchi", [1, 1024], BF16)
    rclo = L.sb("rclo", [1, 1024], BF16)
    rcf = L.sb("rcf", [1, 1024], F32)
    bcs = L.sb("bcs", [128, 512], F32)
    oout = L.sb("oout", [128, 512], BF16)

    psa = L.ps("psa", [128, 512])
    psb = L.ps("psb", [128, 512])
    poa = L.ps("poa", [128, 512])
    pob = L.ps("pob", [128, 512])
    pbc = L.ps("pbc", [128, 512])

    ld = L.ctr("ld")
    mm = L.ctr("mm")
    ssx = L.ctr("ssx")
    vvx = L.ctr("vvx")
    st = L.ctr("st")

    blk = L.begin()

    def on(engf, f):
        engf(f)

    on(blk.gpsimd, lambda g: ld.inc(g.dma_start(out=onesb[:], in_=ones[:]), 16))

    NJ = S // 128
    for u in range(nunits):
        def s_ldu(s, u=u):
            s.dma_start(out=qs[:], in_=qd[u, :, :]).then_inc(ld.sem, 16)
            s.dma_start(out=ks[:], in_=kd[u, :, :]).then_inc(ld.sem, 16)
            ld.v += 32
            for jt in range(NJ):
                s.dma_start(out=vs[:, jt * 130:(jt + 1) * 130],
                            in_=vd[u, jt * 128:(jt + 1) * 128, :]).then_inc(ld.sem, 16)
                ld.v += 16
        on(blk.sync, s_ldu)

        for ic in range(S // 512):
            i0 = ic * 512
            for jt in range(NJ):
                def t_s(te, u=u, ic=ic, jt=jt, i0=i0):
                    if jt == 0 and ic == 0:
                        ld.wait(te)
                    ssx.wait(te)
                    mm.inc(te.matmul(psa[:], ks[0:64, jt * 128:(jt + 1) * 128],
                                     qs[0:64, i0:i0 + 512], start=True, stop=True,
                                     tile_position=(0, 0)))
                    mm.inc(te.matmul(psb[:], ks[64:128, jt * 128:(jt + 1) * 128],
                                     qs[64:128, i0:i0 + 512], start=True, stop=True,
                                     tile_position=(64, 0)))
                on(blk.tensor, t_s)

                def s_exp(sc, jt=jt):
                    mm.wait(sc)
                    ssx.inc(sc.activation(out=pta[:], in_=psa[:], func=AF.Exp, scale=scale))
                    ssx.inc(sc.activation(out=ptb[:], in_=psb[:], func=AF.Exp, scale=scale))
                on(blk.scalar, s_exp)

                def t_pv(te, jt=jt):
                    if jt == 0:
                        vvx.wait(te)
                    ssx.wait(te)
                    mm.inc(te.matmul(poa[0:65, :], vs[:, jt * 130:jt * 130 + 65],
                                     pta[:], start=(jt == 0), stop=(jt == NJ - 1)))
                    mm.inc(te.matmul(pob[0:65, :], vs[:, jt * 130 + 65:jt * 130 + 130],
                                     ptb[:], start=(jt == 0), stop=(jt == NJ - 1)))
                on(blk.tensor, t_pv)

            for hh, po in ((0, poa), (1, pob)):
                def v_rc(v, hh=hh, po=po, ic=ic):
                    mm.wait(v)
                    vvx.inc(v.reciprocal(out=rcf[0:1, hh * 512:hh * 512 + 512], in_=po[64:65, :]))
                    vvx.wait(v)
                    vvx.inc(v.tensor_copy(out=rchi[0:1, 0:512], in_=rcf[0:1, hh * 512:hh * 512 + 512]))
                    vvx.wait(v)
                    vvx.inc(v.tensor_tensor(out=rclo[0:1, 0:512], in0=rcf[0:1, hh * 512:hh * 512 + 512],
                                            in1=rchi[0:1, 0:512], op=TT.subtract))
                on(blk.vector, v_rc)

                def t_bc(te, hh=hh):
                    vvx.wait(te)
                    te.matmul(pbc[0:64, :], onesb[:], rchi[0:1, 0:512], start=True, stop=False)
                    mm.inc(te.matmul(pbc[0:64, :], onesb[:], rclo[0:1, 0:512], start=False, stop=True))
                on(blk.tensor, t_bc)

                def v_norm(v, hh=hh, po=po):
                    mm.wait(v)
                    vvx.inc(v.tensor_copy(out=bcs[0:64, :], in_=pbc[0:64, :]))
                    vvx.wait(v)
                    vvx.inc(v.tensor_tensor(out=oout[0:64, :], in0=po[0:64, :],
                                            in1=bcs[0:64, :], op=TT.mult))
                on(blk.vector, v_norm)

                def s_sto(s, u=u, hh=hh, i0=i0):
                    vvx.wait(s)
                    st.inc(s.dma_start(out=od[u, hh * 64:(hh + 1) * 64, i0:i0 + 512],
                                       in_=oout[0:64, :]), 16)
                    st.wait(s)
                on(blk.sync, s_sto)

    return L.done()


def l02_inmaps(qf, kf, vf):
    ones = np.ones((1, 64), dtype=BF)
    ims = []
    for r in range(NCORES):
        qa = np.zeros((4, 128, S), dtype=BF)
        ka = np.zeros((4, 128, S), dtype=BF)
        va = np.zeros((4, S, 130), dtype=BF)
        for un in range(4):
            bidx = (r * 4 + un) // 8
            hp = (r * 4 + un) % 8
            rows = slice(bidx * S, (bidx + 1) * S)
            for a in range(2):
                hh = hp * 2 + a
                qh = np.concatenate([qf[rows, 32 * hh:32 * hh + 32],
                                     qf[rows, 512 + 32 * hh:512 + 32 * hh + 32]], 1)
                kh = np.concatenate([kf[rows, 32 * hh:32 * hh + 32],
                                     kf[rows, 512 + 32 * hh:512 + 32 * hh + 32]], 1)
                qa[un, a * 64:(a + 1) * 64] = qh.T
                ka[un, a * 64:(a + 1) * 64] = kh.T
                va[un, :, a * 65:a * 65 + 64] = vf[rows, 64 * hh:64 * hh + 64]
                va[un, :, a * 65 + 64] = np.ones(S, dtype=BF)
        ims.append({"q": qa, "k": ka, "v": va, "ones1": ones})
    return ims



def build_b0_out():
    L = LP("b0out")
    nc = L.nc
    ofm = L.inp("ofm", [8, 128, TPC], BF16)
    wo = L.inp("wo", [D, D], F32)
    bo = L.inp("bo", [1, D], F32)
    xin = L.inp("x", [TPC, D], F32)
    x1_o = L.out("x1", [TPC, D], F32)
    y0_o = L.out("y0", [TPC, D], F32)

    wsb = L.sb("wsb", [128, 8 * D], BF16)
    bbc = L.sb("bbc", [128, D], F32)
    osb = L.sb("osb", [128, 8 * 128], BF16)
    xt = L.sb("xt", [128, D], F32)
    x1t = L.sb("x1t", [128, D], F32)
    y0t = L.sb("y0t", [128, D], F32)
    stats = L.sb("stats", [128, 12], F32)
    aggr = L.sb("aggr", [128, 2], F32)
    lnt = L.sb("lnt", [128, 2], F32)
    epsc = L.sb("epsc", [128, 1], F32)
    py = L.ps("py", [128, 1024])

    ldw = L.ctr("ldw")
    ldx = L.ctr("ldx")
    mm = L.ctr("mm")
    vv = L.ctr("vv")
    ss = L.ctr("ss")
    st = L.ctr("st")
    blk = L.begin()

    def on(engf, f):
        engf(f)

    def g_loads(g):
        _load_weight_chunks(g, ldw, wsb, wo, D, D)
        ldw.inc(g.dma_start(out=bbc[:], in_=bo.to_broadcast([128, D])), 16)
    on(blk.gpsimd, g_loads)
    on(blk.vector, lambda v: vv.inc(v.memset(epsc[:], 1e-5)))

    NT = TPC // 128
    for t in range(NT):
        r0 = t * 128

        def s_ld(s, r0=r0, t=t):
            for c in range(8):
                s.dma_start(out=osb[:, c * 128:(c + 1) * 128],
                            in_=ofm[c, :, r0:r0 + 128]).then_inc(ldx.sem, 16)
                ldx.v += 16
            s.dma_start(out=xt[:], in_=xin[r0:r0 + 128, :]).then_inc(ldx.sem, 16)
            ldx.v += 16
        on(blk.sync, s_ld)

        def t_mm(te, t=t):
            if t == 0:
                ldw.wait(te)
            else:
                vv.wait(te)
            te.wait_ge(ldx.sem, ldx.v)
            for half in range(2):
                for c in range(8):
                    ins = te.matmul(py[:, half * 512:(half + 1) * 512],
                                    osb[:, c * 128:(c + 1) * 128],
                                    wsb[:, c * D + half * 512:c * D + half * 512 + 512],
                                    start=(c == 0), stop=(c == 7))
                    if half == 1 and c == 7:
                        mm.inc(ins)
        on(blk.tensor, t_mm)

        def v_out(v):
            mm.wait(v)
            vv.inc(v.tensor_tensor(out=x1t[:], in0=py[:], in1=bbc[:], op=TT.add))
            vv.wait(v)
            vv.inc(v.tensor_tensor(out=x1t[:], in0=x1t[:], in1=xt[:], op=TT.add))
            vv.wait(v)
            vv.inc(v.bn_stats(out=stats[:, 0:6], in_=x1t[:, 0:512]))
            vv.inc(v.bn_stats(out=stats[:, 6:12], in_=x1t[:, 512:1024]))
            vv.wait(v)
            vv.inc(v.bn_aggr(out=aggr[:], in_=stats[:]))
        on(blk.vector, v_out)

        def s_ln(sc):
            vv.wait(sc)
            ss.inc(sc.activation(out=lnt[:, 0:1], in_=aggr[:, 1:2], func=AF.Ln,
                                 bias=epsc[:, 0:1]))
            ss.wait(sc)
            ss.inc(sc.activation(out=lnt[:, 1:2], in_=lnt[:, 0:1], func=AF.Exp,
                                 scale=-0.5))
        on(blk.scalar, s_ln)

        def v_fin(v):
            ss.wait(v)
            vv.inc(v.tensor_scalar(out=y0t[:], in0=x1t[:], scalar1=aggr[:, 0:1],
                                   scalar2=lnt[:, 1:2], op0=TT.subtract, op1=TT.mult))
        on(blk.vector, v_fin)

        def s_st(s, r0=r0):
            vv.wait(s)
            st.inc(s.dma_start(out=x1_o[r0:r0 + 128, :], in_=x1t[:]), 16)
            st.inc(s.dma_start(out=y0_o[r0:r0 + 128, :], in_=y0t[:]), 16)
            st.wait(s)
        on(blk.sync, s_st)

    return L.done()


def l03_inmaps(o_units, W, x_flat):
    Ofull = np.zeros((B, 8, 128, S), dtype=BF)
    for r in range(NCORES):
        for un in range(4):
            bidx = (r * 4 + un) // 8
            hp = (r * 4 + un) % 8
            Ofull[bidx, hp] = o_units[r][un]
    ims = []
    for r in range(NCORES):
        bidx = r // 2
        s0 = (r % 2) * TPC
        ofm = Ofull[bidx, :, :, s0:s0 + TPC]
        ims.append({
            "ofm": np.ascontiguousarray(ofm),
            "wo": f32c(W["L0_attn_Wo_w"]),
            "bo": f32c(W["L0_attn_Wo_b"]).reshape(1, D),
            "x": f32c(x_flat[r * TPC:(r + 1) * TPC]),
        })
    return ims



EXTT = 2
NJT_I = 16 + EXTT


def build_inner_attn():
    L = LP("innera")
    nc = L.nc
    y_ext = L.inp("y_ext", [TPC + 257, D], F32)
    wqkv = L.inp("wqkv", [D, 3 * D], F32)
    wo = L.inp("wo", [D, D], F32)
    qs_cols = L.inp("qs_cols", [128, 4], F32)
    qsrow = L.inp("qsrow", [1, HD], F32)
    ksrow = L.inp("ksrow", [1, HD], F32)
    tq = L.inp("tq", [2, 128, TPC], F32)
    tk = L.inp("tk", [2, 128, TPC + 256], F32)
    eye = L.inp("eye", [128, 128], BF16)
    rmat = L.inp("rmat", [128, 128], BF16)
    m16 = L.inp("m16", [8, 128, 16], BF16)
    h16 = L.inp("h16", [8, 16, 128], BF16)
    ones2 = L.inp("ones2", [1, 128], BF16)
    tric = L.inp("tric", [128, 128], BF16)
    pm_a = L.inp("pm_a", [128, 128], BF16)
    pm_b = L.inp("pm_b", [128, 128], BF16)
    trip = L.inp("trip", [128, 128], BF16)
    o_y = L.out("yo", [TPC, D], F32)

    NE = NJT_I

    wq_sb = L.sb("wq_sb", [128, 8 * 3 * D], BF16)
    wo_sb = L.sb("wo_sb", [128, 8 * D], BF16)
    qsc = L.sb("qsc", [128, 4], F32)
    qsr = L.sb("qsr", [1, 2 * HD], F32)
    cbuf = L.sb("cbuf", [1, 2], F32)
    ccast = L.sb("ccast", [1, 1], BF16)
    cneg = L.sb("cneg", [128, 1], F32)
    tqs = L.sb("tqs", [128, 2 * 128], F32)
    tks = L.sb("tks", [128, 2 * 128], F32)
    eyesb = L.sb("eyesb", [128, 128], BF16)
    rmats = L.sb("rmats", [128, 128], BF16)
    m16s = L.sb("m16s", [128, 8 * 16], BF16)
    h16s = L.sb("h16s", [16, 8 * 128], BF16)
    ones2s = L.sb("ones2s", [1, 128], BF16)
    tris = L.sb("tris", [128, 4 * 128], BF16)
    xst = L.sb("xst", [128, D], BF16)
    xsfm = L.sb("xsfm", [128, 8 * 128], BF16)
    kall = L.sb("kall", [128, NE * 8 * 128], BF16)
    vall = L.sb("vall", [128, NE * 1040], BF16)
    craw = L.sb("craw", [128, 8 * 128], BF16)
    qpr = L.sb("qpr", [128, 8 * 128], BF16)
    sqb = L.sb("sqb", [128, 1024], BF16)
    rnof = L.sb("rnof", [16, 128], F32)
    rno = L.sb("rno", [16, 128], BF16)
    rotq = L.sb("rotq", [128, 128], BF16)
    ub = [L.sb(f"ub{i}", [128, 128], F32) for i in range(2)]
    tb1 = L.sb("tb1", [128, 128], F32)
    tb2 = L.sb("tb2", [128, 128], F32)
    pt0 = L.sb("pt0", [128, 384], BF16)
    pt1 = L.sb("pt1", [128, 384], BF16)
    rc1 = L.sb("rc1", [1, 128], F32)
    rchi2 = L.sb("rchi2", [1, 128], BF16)
    rclo2 = L.sb("rclo2", [1, 128], BF16)
    bc2 = L.sb("bc2", [128, 128], F32)
    och = L.sb("och", [128, 8 * 128], BF16)
    yt = L.sb("yt", [128, D], F32)
    youts = L.sb("youts", [128, D], F32)

    pfm = L.ps("pfm", [128, 128], BF16)
    pfm2 = L.ps("pfm2", [128, 128])
    pS0 = L.ps("pS0", [128, 384])
    pS1 = L.ps("pS1", [128, 384])
    pO = L.ps("pO", [128, 512])
    pW = L.ps("pW", [128, 1024])

    ld = L.ctr("ld")
    lx = L.ctr("lx")
    ly = L.ctr("ly")
    mm = L.ctr("mm")
    vv = L.ctr("vv")
    ss = L.ctr("ss")
    st = L.ctr("st")
    blk = L.begin()

    def on(engf, f):
        engf(f)

    def g_loads(g):
        _load_weight_chunks(g, ld, wq_sb, wqkv, D, 3 * D)
        _load_weight_chunks(g, ld, wo_sb, wo, D, D)
        ld.inc(g.dma_start(out=qsc[:], in_=qs_cols[:]), 16)
        ld.inc(g.dma_start(out=qsr[0:1, 0:HD], in_=qsrow[:]), 16)
        ld.inc(g.dma_start(out=qsr[0:1, HD:2 * HD], in_=ksrow[:]), 16)
        ld.inc(g.dma_start(out=eyesb[:], in_=eye[:]), 16)
        ld.inc(g.dma_start(out=rmats[:], in_=rmat[:]), 16)
        for ci in range(8):
            ld.inc(g.dma_start(out=m16s[:, ci * 16:(ci + 1) * 16], in_=m16[ci]), 16)
            ld.inc(g.dma_start(out=h16s[:, ci * 128:(ci + 1) * 128], in_=h16[ci]), 16)
        ld.inc(g.dma_start(out=ones2s[:], in_=ones2[:]), 16)
        for i, ap in enumerate([tric, trip, pm_a, pm_b]):
            ld.inc(g.dma_start(out=tris[:, i * 128:(i + 1) * 128], in_=ap), 16)
    on(blk.gpsimd, g_loads)

    def v_ones2(v):
        ld.wait(v)
        for jt in range(NE):
            base = jt * 1040 + 64
            vv.inc(v.memset(vall[:, base:base + 15 * 65 + 1:65], 1.0))
    on(blk.vector, v_ones2)

    def v_c(v):
        vv.inc(v.tensor_tensor(out=qsr[0:1, HD:2 * HD], in0=qsr[0:1, 0:HD],
                               in1=qsr[0:1, HD:2 * HD], op=TT.mult))
        vv.wait(v)
        vv.inc(v.tensor_reduce(out=cbuf[0:1, 0:1], in_=qsr[0:1, HD:2 * HD],
                               axis=mybir.AxisListType.X, op=TT.max,
                               apply_absolute_value=True))
        vv.wait(v)
        vv.inc(v.tensor_scalar(out=cbuf[0:1, 1:2], in0=cbuf[0:1, 0:1], scalar1=-8.0,
                               scalar2=None, op0=TT.mult))
        vv.wait(v)
        vv.inc(v.tensor_copy(out=ccast[:], in_=cbuf[0:1, 1:2]))
    on(blk.vector, v_c)

    def t_cbc(te):
        ld.wait(te)
        vv.wait(te)
        mm.inc(te.matmul(pS0[:, 0:1], ones2s[:], ccast[:], start=True, stop=True))
    on(blk.tensor, t_cbc)
    on(blk.vector, lambda v: (mm.wait(v), vv.inc(v.tensor_copy(out=cneg[:], in_=pS0[:, 0:1])))[-1])

    def emit_fm_xs(tau0):
        def g_ld(g):
            ss.wait(g)
            g.dma_start(out=xst[:, 0:512], in_=y_ext[tau0 + 1:tau0 + 129, 0:512]).then_inc(lx.sem, 16)
            g.dma_start(out=xst[:, 512:1024], in_=y_ext[tau0:tau0 + 128, 512:1024]).then_inc(lx.sem, 16)
            lx.v += 32
        on(blk.gpsimd, g_ld)
        for c in range(8):
            def t_tr(te, c=c):
                ss.wait(te)
                if c == 0:
                    lx.wait(te)
                mm.inc(te.transpose(pfm[:], xst[:, c * 128:(c + 1) * 128], eyesb[:]))
            on(blk.tensor, t_tr)

            def s_ev(sc, c=c):
                mm.wait(sc)
                ss.inc(sc.copy(out=xsfm[:, c * 128:(c + 1) * 128], in_=pfm[:]))
            on(blk.scalar, s_ev)

    def emit_proj_chunks(dst, col0, nch, gate_vv=False):
        if gate_vv:
            def s_gate(sc):
                vv.wait(sc)
            on(blk.scalar, s_gate)
        for oc in range(nch):
            def t_mm(te, oc=oc):
                ss.wait(te)
                if oc == 0:
                    vv.wait(te)
                for c in range(8):
                    ins = te.matmul(pfm2[:], wq_sb[:, c * 3 * D + col0 + oc * 128:
                                                  c * 3 * D + col0 + oc * 128 + 128],
                                    xsfm[:, c * 128:(c + 1) * 128],
                                    start=(c == 0), stop=(c == 7))
                    if c == 7:
                        mm.inc(ins)
            on(blk.tensor, t_mm)

            def s_ev(sc, oc=oc):
                mm.wait(sc)
                ss.inc(sc.copy(out=dst[:, oc * 128:(oc + 1) * 128], in_=pfm2[:]))
            on(blk.scalar, s_ev)

    def emit_qk_process(src, dst, side, tslab):
        qcol = qsc[:, 2 * side:2 * side + 1]
        qrot = qsc[:, 2 * side + 1:2 * side + 2]

        def s_sq(sc):
            ss.wait(sc)
            ss.inc(sc.activation(out=sqb[:, 0:512], in_=src[:, 0:512], func=AF.Square))
            ss.inc(sc.activation(out=sqb[:, 512:1024], in_=src[:, 512:1024], func=AF.Square))
        on(blk.scalar, s_sq)

        def t_ss(te):
            ss.wait(te)
            vv.wait(te)
            for c in range(8):
                ins = te.matmul(pS1[0:16, 0:128], m16s[:, c * 16:(c + 1) * 16],
                                sqb[:, c * 128:(c + 1) * 128],
                                start=(c == 0), stop=(c == 7))
                if c == 7:
                    mm.inc(ins)
        on(blk.tensor, t_ss)

        def v_rn(v):
            mm.wait(v)
            vv.inc(v.reciprocal(out=rnof[0:16, :], in_=pS1[0:16, 0:128]))
        on(blk.vector, v_rn)

        def s_rn(sc):
            vv.wait(sc)
            ss.inc(sc.activation(out=rno[0:16, :], in_=rnof[0:16, :], func=AF.Sqrt))
        on(blk.scalar, s_rn)

        for c in range(8):
            def t_b(te, c=c):
                ss.wait(te)
                vv.wait(te)
                mm.inc(te.matmul(pS0[:, 0:128], h16s[:, c * 128:(c + 1) * 128],
                                 rno[0:16, :], start=True, stop=True))
                mm.inc(te.matmul(pfm2[:], rmats[:], src[:, c * 128:(c + 1) * 128],
                                 start=True, stop=True))
            on(blk.tensor, t_b)

            def s_rot(sc):
                mm.wait(sc)
                ss.inc(sc.copy(out=rotq[:], in_=pfm2[:]))
            on(blk.scalar, s_rot)

            def v_u(v, c=c):
                ss.wait(v)
                mm.wait(v)
                csA = tslab[:, 0:128]
                ssA = tslab[:, 128:256]
                vv.inc(v.scalar_tensor_tensor(out=ub[0][:], in0=pS0[:, 0:128], scalar=qcol,
                                              in1=csA, op0=TT.mult, op1=TT.mult))
                vv.inc(v.scalar_tensor_tensor(out=ub[1][:], in0=pS0[:, 0:128], scalar=qrot,
                                              in1=ssA, op0=TT.mult, op1=TT.mult))
                vv.wait(v)
                sl = src[:, c * 128:(c + 1) * 128]
                dl = dst[:, c * 128:(c + 1) * 128]
                vv.inc(v.tensor_tensor(out=tb1[:], in0=sl, in1=ub[0][:], op=TT.mult))
                vv.inc(v.tensor_tensor(out=tb2[:], in0=rotq[:], in1=ub[1][:], op=TT.mult))
                vv.wait(v)
                vv.inc(v.tensor_tensor(out=dl, in0=tb1[:], in1=tb2[:], op=TT.add))
            on(blk.vector, v_u)

    for tau in range(NE):
        tau0 = tau * 128
        it = tau - EXTT

        emit_fm_xs(tau0)

        def s_tab(s, tau0=tau0, it=it):
            vv.wait(s)
            s.dma_start(out=tks[:, 0:128], in_=tk[0, :, tau0:tau0 + 128]).then_inc(ly.sem, 16)
            s.dma_start(out=tks[:, 128:256], in_=tk[1, :, tau0:tau0 + 128]).then_inc(ly.sem, 16)
            ly.v += 32
            if it >= 0:
                i0 = it * 128
                s.dma_start(out=tqs[:, 0:128], in_=tq[0, :, i0:i0 + 128]).then_inc(ly.sem, 16)
                s.dma_start(out=tqs[:, 128:256], in_=tq[1, :, i0:i0 + 128]).then_inc(ly.sem, 16)
                s.dma_start(out=yt[:], in_=y_ext[257 + i0:257 + i0 + 128, :]).then_inc(ly.sem, 16)
                ly.v += 48
        on(blk.sync, s_tab)

        emit_proj_chunks(craw, D, 8)

        def v_wt(v):
            v.wait_ge(ly.sem, ly.v)
        on(blk.vector, v_wt)
        emit_qk_process(craw, kall[:, tau * 1024:(tau + 1) * 1024], 1, tks)

        def t_v(te, tau=tau):
            vv.wait(te)
            ss.wait(te)
            for half in range(2):
                for c in range(8):
                    ins = te.matmul(pW[:, half * 512:(half + 1) * 512],
                                    xsfm[:, c * 128:(c + 1) * 128],
                                    wq_sb[:, c * 3 * D + 2 * D + half * 512:
                                          c * 3 * D + 2 * D + half * 512 + 512],
                                    start=(c == 0), stop=(c == 7))
                    if half == 1 and c == 7:
                        mm.inc(ins)
        on(blk.tensor, t_v)

        def v_vev(v, tau=tau):
            mm.wait(v)
            base = tau * 1040
            for hh in range(16):
                vv.inc(v.tensor_copy(out=vall[:, base + hh * 65:base + hh * 65 + 64],
                                     in_=pW[:, hh * 64:(hh + 1) * 64]))
        on(blk.vector, v_vev)

        if it < 0:
            continue

        emit_proj_chunks(craw, 0, 8, gate_vv=True)
        emit_qk_process(craw, qpr, 0, tqs)

        for c in range(8):
            def t_s(te, c=c, it=it):
                ss.wait(te)
                if c == 0:
                    vv.wait(te)
                for jr in range(3):
                    jt = it + jr
                    te_i = te.matmul(pS0[:, jr * 128:(jr + 1) * 128],
                                     kall[0:64, jt * 1024 + c * 128:jt * 1024 + (c + 1) * 128],
                                     qpr[0:64, c * 128:(c + 1) * 128],
                                     start=True, stop=True, tile_position=(0, 0))
                    ins = te.matmul(pS1[:, jr * 128:(jr + 1) * 128],
                                    kall[64:128, jt * 1024 + c * 128:jt * 1024 + (c + 1) * 128],
                                    qpr[64:128, c * 128:(c + 1) * 128],
                                    start=True, stop=True, tile_position=(64, 0))
                    if jr == 2:
                        mm.inc(ins)
            on(blk.tensor, t_s)

            def s_exp(sc, c=c):
                mm.wait(sc)
                vv.wait(sc)
                ss.inc(sc.activation(out=pt0[:], in_=pS0[:, 0:384], func=AF.Exp,
                                     scale=8.0, bias=cneg[:, 0:1]))
                ss.inc(sc.activation(out=pt1[:], in_=pS1[:, 0:384], func=AF.Exp,
                                     scale=8.0, bias=cneg[:, 0:1]))
            on(blk.scalar, s_exp)

            def v_mk(v, c=c, it=it):
                ss.wait(v)
                for par, ptb in ((0, pt0), (1, pt1)):
                    mk0 = tris[:, 2 * 128:3 * 128] if it <= 1 else tris[:, 128:256]
                    vv.inc(v.tensor_tensor(out=ptb[:, 0:128], in0=ptb[:, 0:128],
                                           in1=mk0, op=TT.mult))
                    if it == 0:
                        vv.inc(v.tensor_tensor(out=ptb[:, 128:256], in0=ptb[:, 128:256],
                                               in1=tris[:, 3 * 128:4 * 128], op=TT.mult))
                    vv.inc(v.tensor_tensor(out=ptb[:, 256:384], in0=ptb[:, 256:384],
                                           in1=tris[:, 0:128], op=TT.mult))
            on(blk.vector, v_mk)

            def t_pv(te, c=c, it=it):
                vv.wait(te)
                for par, ptb in ((0, pt0), (1, pt1)):
                    hh = 2 * c + par
                    slot = hh % 4
                    for jr in range(3):
                        jt = it + jr
                        ins = te.matmul(pO[0:65, slot * 128:(slot + 1) * 128],
                                        vall[:, jt * 1040 + hh * 65:jt * 1040 + hh * 65 + 65],
                                        ptb[:, jr * 128:(jr + 1) * 128],
                                        start=(jr == 0), stop=(jr == 2))
                        if jr == 2:
                            mm.inc(ins)
            on(blk.tensor, t_pv)

            for par in range(2):
                hh = 2 * c + par
                slot = hh % 4

                def v_rc(v, slot=slot):
                    mm.wait(v)
                    vv.inc(v.reciprocal(out=rc1[:], in_=pO[64:65, slot * 128:(slot + 1) * 128]))
                    vv.wait(v)
                    vv.inc(v.tensor_copy(out=rchi2[:], in_=rc1[:]))
                    vv.wait(v)
                    vv.inc(v.tensor_tensor(out=rclo2[:], in0=rc1[:], in1=rchi2[:],
                                           op=TT.subtract))
                on(blk.vector, v_rc)

                def t_bc(te):
                    vv.wait(te)
                    te.matmul(pfm2[:], ones2s[:], rchi2[:], start=True, stop=False)
                    mm.inc(te.matmul(pfm2[:], ones2s[:], rclo2[:], start=False, stop=True))
                on(blk.tensor, t_bc)

                def v_nm(v, hh=hh, slot=slot):
                    mm.wait(v)
                    vv.inc(v.tensor_copy(out=bc2[:], in_=pfm2[:]))
                    vv.wait(v)
                    half = hh % 2
                    vv.inc(v.tensor_tensor(
                        out=och[64 * half:64 * half + 64, (hh // 2) * 128:(hh // 2 + 1) * 128],
                        in0=pO[0:64, slot * 128:(slot + 1) * 128], in1=bc2[0:64, :], op=TT.mult))
                on(blk.vector, v_nm)

        def t_wo(te, it=it):
            vv.wait(te)
            for half in range(2):
                for c in range(8):
                    ins = te.matmul(pW[:, half * 512:(half + 1) * 512],
                                    och[:, c * 128:(c + 1) * 128],
                                    wo_sb[:, c * D + half * 512:c * D + half * 512 + 512],
                                    start=(c == 0), stop=(c == 7))
                    if half == 1 and c == 7:
                        mm.inc(ins)
        on(blk.tensor, t_wo)

        def v_res(v, it=it):
            mm.wait(v)
            v.wait_ge(ly.sem, ly.v)
            vv.inc(v.tensor_tensor(out=youts[:], in0=pW[:], in1=yt[:], op=TT.add))
        on(blk.vector, v_res)

        def s_sty(s, it=it):
            vv.wait(s)
            st.inc(s.dma_start(out=o_y[it * 128:(it + 1) * 128, :], in_=youts[:]), 16)
            st.wait(s)
        on(blk.sync, s_sty)

    return L.done()


def _inner_attn_consts():
    il = np.arange(128)[None, :]
    jl = np.arange(128)[:, None]
    tric = (jl <= il).astype(np.float32).astype(BF)
    trip = (jl >= il).astype(np.float32).astype(BF)
    full0 = np.zeros((128, 128), dtype=BF)
    ones_m = np.ones((128, 128), dtype=BF)
    m16 = np.zeros((8, 128, 16), dtype=BF)
    h16 = np.zeros((8, 16, 128), dtype=BF)
    for c in range(8):
        for p in range(128):
            hh = 2 * c + p // 64
            m16[c, p, hh] = 1
            h16[c, hh, p] = 1
    ones2 = np.ones((1, 128), dtype=BF)
    rmat = np.zeros((128, 128), dtype=np.float32)
    for base in (0, 64):
        for d in range(32):
            rmat[base + 32 + d, base + d] = -1.0
            rmat[base + d, base + 32 + d] = 1.0
    return tric, trip, full0, ones_m, m16, h16, ones2, rmat.astype(BF)


def _xpos_freq_scale():
    half = np.arange(0, HD, 2, dtype=np.float32)
    inv_freq = (1.0 / 10000.0 ** (half / np.float32(HD))).astype(np.float32)
    t = np.arange(S, dtype=np.float32)
    fr = (t[:, None] * inv_freq[None, :]).astype(np.float32)
    base_scale = ((half + 0.4 * HD) / (1.4 * HD)).astype(np.float32)
    power = ((t - S // 2) / np.float32(WIN // 2)).astype(np.float32)
    sc = (base_scale[None, :] ** power[:, None]).astype(np.float32)
    return fr, sc


def l04_inmaps(y_flat, Wqkv, qs, ks, Wo):
    tric, trip, full0, ones_m, m16, h16, ones2, rmat = _inner_attn_consts()
    fr, sc = _xpos_freq_scale()
    cosf, sinf = np.cos(fr).astype(np.float32), np.sin(fr).astype(np.float32)
    csA = (cosf * sc).astype(np.float32)
    ssA = (sinf * sc).astype(np.float32)
    csI = (cosf / sc).astype(np.float32)
    ssI = (sinf / sc).astype(np.float32)
    qs_cols = np.zeros((128, 4), np.float32)
    p64 = np.arange(128) % 64
    qs_cols[:, 0] = qs[p64]
    qs_cols[:, 1] = qs[(p64 + 32) % 64]
    qs_cols[:, 2] = ks[p64]
    qs_cols[:, 3] = ks[(p64 + 32) % 64]
    p32 = np.arange(128) % 32
    ims = []
    for r in range(NCORES):
        b = r // 2
        s0 = (r % 2) * TPC
        ye = np.empty((TPC + 257, D), np.float32)
        if r % 2 == 0:
            ye[0:256] = 1.0
            ye[256] = 0.0
            ye[257:] = y_flat[b * S:(b * S) + TPC]
        else:
            ye[:] = y_flat[b * S + s0 - 257:b * S + s0 + TPC]
        tql = np.stack([csA[s0:s0 + TPC][:, p32].T, ssA[s0:s0 + TPC][:, p32].T])
        tidx = np.clip(np.arange(s0 - 256, s0 + TPC), 0, S - 1)
        tkl = np.stack([csI[tidx][:, p32].T, ssI[tidx][:, p32].T])
        ims.append({
            "y_ext": ye,
            "wqkv": f32c(Wqkv),
            "wo": f32c(Wo),
            "qs_cols": qs_cols,
            "qsrow": f32c(qs).reshape(1, HD),
            "ksrow": f32c(ks).reshape(1, HD),
            "tq": np.ascontiguousarray(tql),
            "tk": np.ascontiguousarray(tkl),
            "eye": _EYE,
            "rmat": rmat,
            "m16": m16,
            "h16": h16,
            "ones2": ones2,
            "tric": tric,
            "pm_a": full0 if r % 2 == 0 else trip,
            "pm_b": full0 if r % 2 == 0 else ones_m,
            "trip": trip,
        })
    return ims



N1CH = 44
N2CH = 22


def _w1_chunks():
    out = []
    for m in range(22):
        w = 128 if m < 21 else 42
        out.append((m * 128, w))
    return out


def build_inner_ffn(extra=False):
    L = LP("innerf")
    nc = L.nc
    y_ext = L.inp("y_ext", [TPC + 1, D], F32)
    w1 = L.inp("w1", [D, 2 * DFH], F32)
    w2 = L.inp("w2", [DFH, D], F32)
    b1f = L.inp("b1f", [128, N1CH], F32)
    b2r = L.inp("b2r", [1, D], F32)
    lngr = L.inp("lngr", [1, D], F32)
    lnbr = L.inp("lnbr", [1, D], F32)
    eye = L.inp("eye", [128, 128], BF16)
    if extra:
        xtr = L.inp("xtr", [TPC, D], F32)
    o_y = L.out("yo", [TPC, D], F32)

    w1_sb = L.sb("w1_sb", [128, 8 * 2 * DFH], BF16)
    w2_sb = L.sb("w2_sb", [128, N2CH * D], BF16)
    b1s = L.sb("b1s", [128, N1CH], F32)
    b2bc = L.sb("b2bc", [128, D], F32)
    gbc = L.sb("gbc", [128, D], F32)
    bbc = L.sb("bbc", [128, D], F32)
    eyesb = L.sb("eyesb", [128, 128], BF16)
    xsf = L.sb("xsf", [128, D], F32)
    xnb = L.sb("xnb", [128, D], BF16)
    t1 = L.sb("t1", [128, D], F32)
    stats = L.sb("stats", [128, 12], F32)
    aggr = L.sb("aggr", [128, 2], F32)
    lnt = L.sb("lnt", [128, 2], F32)
    epsc = L.sb("epsc", [128, 1], F32)
    xsfm = L.sb("xsfm", [128, 8 * 128], BF16)
    asb = L.sb("asb", [128, 128], BF16)
    gsb = L.sb("gsb", [128, 128], BF16)
    actsb = L.sb("actsb", [128, N2CH * 128], BF16)
    yt = L.sb("yt", [128, D], F32)
    xtt = L.sb("xtt", [128, D], F32)
    youts = L.sb("youts", [128, D], F32)

    pfm = L.ps("pfm", [128, 128], BF16)
    pfm2 = L.ps("pfm2", [128, 128])
    pW = L.ps("pW", [128, 1024])

    ld = L.ctr("ld")
    lx = L.ctr("lx")
    ly = L.ctr("ly")
    mm = L.ctr("mm")
    vv = L.ctr("vv")
    ss = L.ctr("ss")
    st = L.ctr("st")
    blk = L.begin()

    def on(engf, f):
        engf(f)

    def g_loads(g):
        _load_weight_chunks(g, ld, w1_sb, w1, D, 2 * DFH)
        for c in range(N2CH):
            r0, w = c * 128, (128 if c < 21 else DFH - 21 * 128)
            ld.inc(g.dma_start(out=w2_sb[0:w, c * D:(c + 1) * D],
                               in_=w2[r0:r0 + w, :]), 16)
        ld.inc(g.dma_start(out=b1s[:], in_=b1f[:]), 16)
        ld.inc(g.dma_start(out=b2bc[:], in_=b2r.to_broadcast([128, D])), 16)
        ld.inc(g.dma_start(out=gbc[:], in_=lngr.to_broadcast([128, D])), 16)
        ld.inc(g.dma_start(out=bbc[:], in_=lnbr.to_broadcast([128, D])), 16)
        ld.inc(g.dma_start(out=eyesb[:], in_=eye[:]), 16)
    on(blk.gpsimd, g_loads)
    on(blk.vector, lambda v: vv.inc(v.memset(epsc[:], 1e-5)))

    NT = TPC // 128
    for t in range(NT):
        r0 = t * 128

        def s_ld(s, r0=r0):
            st.wait(s)
            s.dma_start(out=xsf[:, 0:512], in_=y_ext[r0 + 1:r0 + 129, 0:512]).then_inc(lx.sem, 16)
            s.dma_start(out=xsf[:, 512:1024], in_=y_ext[r0:r0 + 128, 512:1024]).then_inc(lx.sem, 16)
            s.dma_start(out=yt[:], in_=y_ext[r0 + 1:r0 + 129, :]).then_inc(lx.sem, 16)
            lx.v += 48
            if extra:
                s.dma_start(out=xtt[:], in_=xtr[r0:r0 + 128, :]).then_inc(lx.sem, 16)
                lx.v += 16
        on(blk.sync, s_ld)

        def v_ln(v, t=t):
            if t == 0:
                ld.wait(v)
            v.wait_ge(lx.sem, lx.v)
            vv.inc(v.bn_stats(out=stats[:, 0:6], in_=xsf[:, 0:512]))
            vv.inc(v.bn_stats(out=stats[:, 6:12], in_=xsf[:, 512:1024]))
            vv.wait(v)
            vv.inc(v.bn_aggr(out=aggr[:], in_=stats[:]))
        on(blk.vector, v_ln)

        def s_ln(sc):
            vv.wait(sc)
            ss.inc(sc.activation(out=lnt[:, 0:1], in_=aggr[:, 1:2], func=AF.Ln,
                                 bias=epsc[:, 0:1]))
            ss.wait(sc)
            ss.inc(sc.activation(out=lnt[:, 1:2], in_=lnt[:, 0:1], func=AF.Exp,
                                 scale=-0.5))
        on(blk.scalar, s_ln)

        def v_xn(v):
            ss.wait(v)
            vv.inc(v.tensor_scalar(out=t1[:], in0=xsf[:], scalar1=aggr[:, 0:1],
                                   scalar2=lnt[:, 1:2], op0=TT.subtract, op1=TT.mult))
            vv.wait(v)
            vv.inc(v.tensor_tensor(out=t1[:], in0=t1[:], in1=gbc[:], op=TT.mult))
            vv.wait(v)
            vv.inc(v.tensor_tensor(out=xnb[:], in0=t1[:], in1=bbc[:], op=TT.add))
        on(blk.vector, v_xn)

        for c in range(8):
            def t_tr(te, c=c):
                ss.wait(te)
                vv.wait(te)
                mm.inc(te.transpose(pfm[:], xnb[:, c * 128:(c + 1) * 128], eyesb[:]))
            on(blk.tensor, t_tr)

            def s_ev(sc, c=c):
                mm.wait(sc)
                ss.inc(sc.copy(out=xsfm[:, c * 128:(c + 1) * 128], in_=pfm[:]))
            on(blk.scalar, s_ev)

        for m in range(N2CH):
            w = 128 if m < 21 else 42
            for which in range(2):
                oc = 2 * m + which

                co = min(oc * 128, 21 * 256 + (oc - 42) * 42 if oc >= 42 else oc * 128)

                def t_mm(te, oc=oc, w=w, co=co):
                    ss.wait(te)
                    for c in range(8):
                        ins = te.matmul(pfm2[0:w, :],
                                        w1_sb[:, c * 2 * DFH + co:c * 2 * DFH + co + w],
                                        xsfm[:, c * 128:(c + 1) * 128],
                                        start=(c == 0), stop=(c == 7))
                        if c == 7:
                            mm.inc(ins)
                on(blk.tensor, t_mm)

                def s_ev2(sc, oc=oc, which=which, w=w):
                    mm.wait(sc)
                    dst = asb if which == 0 else gsb
                    fn = AF.Identity if which == 0 else AF.Gelu
                    ss.inc(sc.activation(out=dst[0:w, 0:128], in_=pfm2[0:w, :], func=fn,
                                         bias=b1s[0:w, oc:oc + 1]))
                on(blk.scalar, s_ev2)

            def v_mul(v, m=m, w=w):
                ss.wait(v)
                vv.inc(v.tensor_tensor(out=actsb[0:w, m * 128:m * 128 + 128],
                                       in0=asb[0:w, 0:128], in1=gsb[0:w, 0:128], op=TT.mult))
            on(blk.vector, v_mul)

        def t_w2(te, t=t):
            vv.wait(te)
            for half in range(2):
                for c in range(N2CH):
                    w = 128 if c < 21 else 42
                    ins = te.matmul(pW[:, half * 512:(half + 1) * 512],
                                    actsb[0:w, c * 128:c * 128 + 128],
                                    w2_sb[0:w, c * D + half * 512:c * D + half * 512 + 512],
                                    start=(c == 0), stop=(c == N2CH - 1))
                    if half == 1 and c == N2CH - 1:
                        mm.inc(ins)
        on(blk.tensor, t_w2)

        def v_out(v):
            mm.wait(v)
            vv.inc(v.tensor_tensor(out=youts[:], in0=pW[:], in1=b2bc[:], op=TT.add))
            vv.wait(v)
            vv.inc(v.tensor_tensor(out=youts[:], in0=youts[:], in1=yt[:], op=TT.add))
            if extra:
                vv.wait(v)
                vv.inc(v.tensor_tensor(out=youts[:], in0=youts[:], in1=xtt[:], op=TT.add))
        on(blk.vector, v_out)

        def s_st(s, r0=r0):
            vv.wait(s)
            st.inc(s.dma_start(out=o_y[r0:r0 + 128, :], in_=youts[:]), 16)
            st.wait(s)
        on(blk.sync, s_st)

    return L.done()


def l05_inmaps(y_flat, lng, lnb, W1, b1, W2, b2, x_extra=None):
    perm = []
    for m in range(22):
        w = 128 if m < 21 else 42
        perm.extend(range(m * 128, m * 128 + w))
        perm.extend(range(DFH + m * 128, DFH + m * 128 + w))
    perm = np.array(perm)
    w1p = W1[:, perm]
    b1p = b1[perm]
    b1f = np.zeros((128, N1CH), np.float32)
    off = 0
    for m in range(22):
        w = 128 if m < 21 else 42
        for which in range(2):
            oc = 2 * m + which
            b1f[0:w, oc] = b1p[off:off + w]
            off += w
    ims = []
    for r in range(NCORES):
        b = r // 2
        s0 = (r % 2) * TPC
        ye = np.empty((TPC + 1, D), np.float32)
        if r % 2 == 0:
            ye[0] = 0.0
            ye[1:] = y_flat[b * S:b * S + TPC]
        else:
            ye[:] = y_flat[b * S + s0 - 1:b * S + s0 + TPC]
        m = {
            "y_ext": ye,
            "w1": f32c(w1p),
            "w2": f32c(W2),
            "b1f": b1f,
            "b2r": f32c(b2).reshape(1, D),
            "lngr": f32c(lng).reshape(1, D),
            "lnbr": f32c(lnb).reshape(1, D),
            "eye": _EYE,
        }
        if x_extra is not None:
            m["xtr"] = f32c(x_extra[r * TPC:(r + 1) * TPC])
        ims.append(m)
    return ims





def build_b1_qkv():
    L = LP("b1qkv")
    nc = L.nc
    x = L.inp("x", [TPC, D], F32)
    eye = L.inp("eye", [128, 128], BF16)
    wq = L.inp("wq", [D, D], F32)
    wk = L.inp("wk", [D, D], F32)
    wv = L.inp("wv", [D, D], F32)
    q_o = L.out("q", [TPC, D], BF16)
    k_o = L.out("k", [TPC, D], BF16)
    v_o = L.out("v", [TPC, D], BF16)

    wqs = L.sb("wqs", [128, 8 * D], BF16)
    wks = L.sb("wks", [128, 8 * D], BF16)
    wvs = L.sb("wvs", [128, 8 * D], BF16)
    eyesb = L.sb("eyesb", [128, 128], BF16)
    xt = L.sb("xt", [128, D], F32)
    xnb = L.sb("xnb", [128, D], BF16)
    stats = L.sb("stats", [128, 12], F32)
    aggr = L.sb("aggr", [128, 2], F32)
    lnt = L.sb("lnt", [128, 2], F32)
    epsc = L.sb("epsc", [128, 1], F32)
    xnfm = L.sb("xnfm", [128, 8 * 128], BF16)
    ob = L.sb("ob", [128, D], BF16)
    pfm = L.ps("pfm", [128, 128], BF16)
    py = L.ps("py", [128, 1024])

    ld = L.ctr("ld")
    lx = L.ctr("lx")
    mm = L.ctr("mm")
    vv = L.ctr("vv")
    ss = L.ctr("ss")
    st = L.ctr("st")
    blk = L.begin()

    def on(engf, f):
        engf(f)

    def g_loads(g):
        _load_weight_chunks(g, ld, wqs, wq, D, D)
        _load_weight_chunks(g, ld, wks, wk, D, D)
        _load_weight_chunks(g, ld, wvs, wv, D, D)
        ld.inc(g.dma_start(out=eyesb[:], in_=eye[:]), 16)
    on(blk.gpsimd, g_loads)
    on(blk.vector, lambda v: vv.inc(v.memset(epsc[:], 1e-5)))

    for t in range(TPC // 128):
        r0 = t * 128

        def s_ld(s, r0=r0):
            st.wait(s)
            s.dma_start(out=xt[:], in_=x[r0:r0 + 128, :]).then_inc(lx.sem, 16)
            lx.v += 16
        on(blk.sync, s_ld)

        def v_ln(v, t=t):
            if t == 0:
                ld.wait(v)
            v.wait_ge(lx.sem, lx.v)
            vv.inc(v.bn_stats(out=stats[:, 0:6], in_=xt[:, 0:512]))
            vv.inc(v.bn_stats(out=stats[:, 6:12], in_=xt[:, 512:1024]))
            vv.wait(v)
            vv.inc(v.bn_aggr(out=aggr[:], in_=stats[:]))
        on(blk.vector, v_ln)

        def s_ln(sc):
            vv.wait(sc)
            ss.inc(sc.activation(out=lnt[:, 0:1], in_=aggr[:, 1:2], func=AF.Ln,
                                 bias=epsc[:, 0:1]))
            ss.wait(sc)
            ss.inc(sc.activation(out=lnt[:, 1:2], in_=lnt[:, 0:1], func=AF.Exp,
                                 scale=-0.5))
        on(blk.scalar, s_ln)

        def v_xn(v):
            ss.wait(v)
            vv.inc(v.tensor_scalar(out=xnb[:], in0=xt[:], scalar1=aggr[:, 0:1],
                                   scalar2=lnt[:, 1:2], op0=TT.subtract, op1=TT.mult))
        on(blk.vector, v_xn)

        for c in range(8):
            def t_tr(te, c=c):
                ss.wait(te)
                if c == 0:
                    vv.wait(te)
                mm.inc(te.transpose(pfm[:], xnb[:, c * 128:(c + 1) * 128], eyesb[:]))
            on(blk.tensor, t_tr)

            def s_ev(sc, c=c):
                mm.wait(sc)
                ss.inc(sc.copy(out=xnfm[:, c * 128:(c + 1) * 128], in_=pfm[:]))
            on(blk.scalar, s_ev)

        for wsb, dsto in ((wqs, q_o), (wks, k_o), (wvs, v_o)):
            def t_mm(te, wsb=wsb):
                ss.wait(te)
                vv.wait(te)
                for half in range(2):
                    for c in range(8):
                        ins = te.matmul(py[:, half * 512:(half + 1) * 512],
                                        xnfm[:, c * 128:(c + 1) * 128],
                                        wsb[:, c * D + half * 512:c * D + half * 512 + 512],
                                        start=(c == 0), stop=(c == 7))
                        if half == 1 and c == 7:
                            mm.inc(ins)
            on(blk.tensor, t_mm)

            def v_ev(v):
                mm.wait(v)
                vv.inc(v.tensor_copy(out=ob[:], in_=py[:]))
            on(blk.vector, v_ev)

            def s_st(s, dsto=dsto, r0=r0):
                vv.wait(s)
                st.inc(s.dma_start(out=dsto[r0:r0 + 128, :], in_=ob[:]), 16)
                st.wait(s)
            on(blk.sync, s_st)

    return L.done()



def build_b1_wv():
    L = LP("b1wv")
    nc = L.nc
    wt = L.inp("wt", [S, 1024], F32)
    vd = L.inp("v", [S, D], BF16)
    wv_o = L.out("wv", [1024, D], F32)

    vsb = L.sb("vsb", [128, 32 * D], BF16)
    wts = L.sb("wts", [128, 32 * 128], BF16)
    ob = L.sb("ob", [128, D], F32)
    py = L.ps("py", [128, 1024])

    ld = L.ctr("ld")
    lw = L.ctr("lw")
    mm = L.ctr("mm")
    vv = L.ctr("vv")
    st = L.ctr("st")
    blk = L.begin()

    def on(engf, f):
        engf(f)

    def g_v(g):
        for jt in range(32):
            ld.inc(g.dma_start(out=vsb[:, jt * D:(jt + 1) * D],
                               in_=vd[jt * 128:(jt + 1) * 128, :]), 16)
    on(blk.gpsimd, g_v)

    for ut in range(8):
        def g_wt(g, ut=ut):
            mm.wait(g)
            for jt in range(32):
                lw.inc(g.dma_start(out=wts[:, jt * 128:(jt + 1) * 128],
                                   in_=wt[jt * 128:(jt + 1) * 128, ut * 128:(ut + 1) * 128]), 16)
        on(blk.gpsimd, g_wt)

        def t_mm(te, ut=ut):
            ld.wait(te)
            lw.wait(te)
            vv.wait(te)
            for half in range(2):
                for jt in range(32):
                    ins = te.matmul(py[:, half * 512:(half + 1) * 512],
                                    wts[:, jt * 128:(jt + 1) * 128],
                                    vsb[:, jt * D + half * 512:jt * D + half * 512 + 512],
                                    start=(jt == 0), stop=(jt == 31))
                    if half == 1 and jt == 31:
                        mm.inc(ins)
        on(blk.tensor, t_mm)

        def v_ev(v):
            mm.wait(v)
            vv.inc(v.tensor_copy(out=ob[:], in_=py[:]))
        on(blk.vector, v_ev)

        def s_st(s, ut=ut):
            vv.wait(s)
            st.inc(s.dma_start(out=wv_o[ut * 128:(ut + 1) * 128, :], in_=ob[:]), 16)
            st.wait(s)
        on(blk.sync, s_st)

    return L.done()



def build_b1_attn():
    L = LP("b1attn")
    nc = L.nc
    qd = L.inp("q", [8, 128, TPC], BF16)
    kd = L.inp("k", [8, 128, S], BF16)
    vd = L.inp("v", [S, 1040], BF16)
    wvd = L.inp("wvd", [2048, D], F32)
    uix = L.inp("uix", [TPC, 16], U32)
    x2d = L.inp("x2", [TPC, D], F32)
    gmr = L.inp("gmr", [1, D], F32)
    btr = L.inp("btr", [1, D], F32)
    rw1 = L.inp("rw1", [D, RH], F32)
    rw2 = L.inp("rw2", [RH, NEXP], F32)
    eye = L.inp("eye", [128, 128], BF16)
    ones = L.inp("ones1", [1, 64], BF16)
    x3_o = L.out("x3", [TPC, D], F32)
    xn3_o = L.out("xn3", [TPC, D], F32)
    lg_o = L.out("lg", [TPC, NEXP], F32)

    qs = L.sb("qs", [128, TPC], BF16)
    ks = L.sb("ks", [128, S], BF16)
    vs = L.sb("vs", [128, 32 * 1040], BF16)
    ofm = L.sb("ofm", [128, 8 * TPC], BF16)
    onesb = L.sb("onesb", [1, 64], BF16)
    eyesb = L.sb("eyesb", [128, 128], BF16)
    gbc = L.sb("gbc", [128, D], F32)
    bbc = L.sb("bbc", [128, D], F32)
    rw1s = L.sb("rw1s", [128, 8 * RH], BF16)
    rw2s = L.sb("rw2s", [128, NEXP], BF16)
    pta = L.sb("pta", [128, 512], BF16)
    ptb = L.sb("ptb", [128, 512], BF16)
    rcf = L.sb("rcf", [1, 1024], F32)
    rchi = L.sb("rchi", [1, 512], BF16)
    rclo = L.sb("rclo", [1, 512], BF16)
    bcs = L.sb("bcs", [128, 512], F32)
    oout = L.sb("oout", [128, 512], BF16)
    uixs = L.sb("uixs", [128, 16], U32)
    gtile = L.sb("gtile", [128, D], F32)
    otm = L.sb("otm", [128, D], F32)
    xt = L.sb("xt", [128, D], F32)
    x3t = L.sb("x3t", [128, D], F32)
    xn3t = L.sb("xn3t", [128, D], F32)
    xn3b = L.sb("xn3b", [128, D], BF16)
    xnfm = L.sb("xnfm", [128, 8 * 128], BF16)
    sil = L.sb("sil", [128, RH], BF16)
    lgt = L.sb("lgt", [128, NEXP], F32)
    stats = L.sb("stats", [128, 12], F32)
    aggr = L.sb("aggr", [128, 2], F32)
    lnt = L.sb("lnt", [128, 2], F32)
    epsc = L.sb("epsc", [128, 1], F32)

    psa = L.ps("psa", [128, 512])
    psb = L.ps("psb", [128, 512])
    poa = L.ps("poa", [128, 512])
    pob = L.ps("pob", [128, 512])
    pbc = L.ps("pbc", [128, 512])
    pfm = L.ps("pfm", [128, 128], BF16)
    prt = L.ps("prt", [128, 128])

    ld = L.ctr("ld")
    lq = L.ctr("lq")
    lg2 = L.ctr("lg2")
    mm = L.ctr("mm")
    vv = L.ctr("vv")
    ss = L.ctr("ss")
    st = L.ctr("st")
    blk = L.begin()

    def on(engf, f):
        engf(f)

    def g_loads(g):
        for jt in range(32):
            ld.inc(g.dma_start(out=vs[:, jt * 1040:(jt + 1) * 1040],
                               in_=vd[jt * 128:(jt + 1) * 128, :]), 16)
        ld.inc(g.dma_start(out=onesb[:], in_=ones[:]), 16)
        ld.inc(g.dma_start(out=eyesb[:], in_=eye[:]), 16)
        ld.inc(g.dma_start(out=gbc[:], in_=gmr.to_broadcast([128, D])), 16)
        ld.inc(g.dma_start(out=bbc[:], in_=btr.to_broadcast([128, D])), 16)
        _load_weight_chunks(g, ld, rw1s, rw1, D, RH)
        ld.inc(g.dma_start(out=rw2s[0:RH, :], in_=rw2[:, :]), 16)
    on(blk.gpsimd, g_loads)
    on(blk.vector, lambda v: vv.inc(v.memset(epsc[:], 1e-5)))

    NJ = S // 128
    scale = float(1.0 / 32.0)

    for hp in range(8):
        def s_ldu(s, hp=hp):
            mm.wait(s)
            s.dma_start(out=qs[:], in_=qd[hp, :, :]).then_inc(lq.sem, 16)
            s.dma_start(out=ks[:], in_=kd[hp, :, :]).then_inc(lq.sem, 16)
            lq.v += 32
        on(blk.sync, s_ldu)

        for ic in range(TPC // 512):
            i0 = ic * 512
            for jt in range(NJ):
                def t_s(te, hp=hp, ic=ic, jt=jt, i0=i0):
                    if jt == 0 and ic == 0:
                        ld.wait(te)
                        lq.wait(te)
                    ssx_wait = ss.wait(te)
                    mm.inc(te.matmul(psa[:], ks[0:64, jt * 128:(jt + 1) * 128],
                                     qs[0:64, i0:i0 + 512], start=True, stop=True,
                                     tile_position=(0, 0)))
                    mm.inc(te.matmul(psb[:], ks[64:128, jt * 128:(jt + 1) * 128],
                                     qs[64:128, i0:i0 + 512], start=True, stop=True,
                                     tile_position=(64, 0)))
                on(blk.tensor, t_s)

                def s_exp(sc):
                    mm.wait(sc)
                    ss.inc(sc.activation(out=pta[:], in_=psa[:], func=AF.Exp, scale=scale))
                    ss.inc(sc.activation(out=ptb[:], in_=psb[:], func=AF.Exp, scale=scale))
                on(blk.scalar, s_exp)

                def t_pv(te, jt=jt, hp=hp):
                    if jt == 0:
                        vv.wait(te)
                    ss.wait(te)
                    for a, po, ptx in ((0, poa, pta), (1, pob, ptb)):
                        hh = 2 * hp + a
                        ins = te.matmul(po[0:65, :], vs[:, jt * 1040 + hh * 65:jt * 1040 + hh * 65 + 65],
                                        ptx[:], start=(jt == 0), stop=(jt == NJ - 1))
                        if a == 1:
                            mm.inc(ins)
                on(blk.tensor, t_pv)

            for a, po in ((0, poa), (1, pob)):
                def v_rc(v, a=a, po=po):
                    mm.wait(v)
                    vv.inc(v.reciprocal(out=rcf[0:1, a * 512:a * 512 + 512], in_=po[64:65, :]))
                    vv.wait(v)
                    vv.inc(v.tensor_copy(out=rchi[0:1, :], in_=rcf[0:1, a * 512:a * 512 + 512]))
                    vv.wait(v)
                    vv.inc(v.tensor_tensor(out=rclo[0:1, :], in0=rcf[0:1, a * 512:a * 512 + 512],
                                           in1=rchi[0:1, :], op=TT.subtract))
                on(blk.vector, v_rc)

                def t_bc(te):
                    vv.wait(te)
                    te.matmul(pbc[0:64, :], onesb[:], rchi[0:1, :], start=True, stop=False)
                    mm.inc(te.matmul(pbc[0:64, :], onesb[:], rclo[0:1, :], start=False, stop=True))
                on(blk.tensor, t_bc)

                def v_nm(v, a=a, po=po, hp=hp, i0=i0):
                    mm.wait(v)
                    vv.inc(v.tensor_copy(out=bcs[0:64, :], in_=pbc[0:64, :]))
                    vv.wait(v)
                    vv.inc(v.tensor_tensor(out=ofm[64 * a:64 * a + 64, hp * TPC + i0:hp * TPC + i0 + 512],
                                           in0=po[0:64, :], in1=bcs[0:64, :], op=TT.mult))
                on(blk.vector, v_nm)

    for t in range(TPC // 128):
        r0 = t * 128

        def s_ld(s, r0=r0, t=t):
            if t > 0:
                st.wait(s)
            s.dma_start(out=xt[:], in_=x2d[r0:r0 + 128, :]).then_inc(lq.sem, 16)
            lq.v += 16
        on(blk.sync, s_ld)

        def g_gather(g, r0=r0, t=t):
            if t > 0:
                st.wait(g)
            g.dma_start(out=uixs[:], in_=uix[r0:r0 + 128, :]).then_inc(lg2.sem, 16)
            lg2.v += 16
            g.wait_ge(lg2.sem, lg2.v)
            for hh in range(16):
                g.indirect_dma_start(
                    out=gtile[:, hh * 64:(hh + 1) * 64],
                    out_offset=None,
                    in_=wvd.rearrange("u (a c) -> (u a) c", c=64),
                    in_offset=IndirectOffsetOnAxis(ap=uixs[:, hh:hh + 1], axis=0),
                ).then_inc(lg2.sem, 16)
                lg2.v += 16
        on(blk.gpsimd, g_gather)

        for c in range(8):
            def t_tr(te, c=c, r0=r0):
                ss.wait(te)
                if c == 0:
                    vv.wait(te)
                mm.inc(te.transpose(pfm[:], ofm[:, c * TPC + r0:c * TPC + r0 + 128], eyesb[:]))
            on(blk.tensor, t_tr)

            def s_ev(sc, c=c):
                mm.wait(sc)
                ss.inc(sc.copy(out=otm[:, c * 128:(c + 1) * 128], in_=pfm[:]))
            on(blk.scalar, s_ev)

        def v_post(v, t=t):
            ss.wait(v)
            v.wait_ge(lg2.sem, lg2.v)
            v.wait_ge(lq.sem, lq.v)
            vv.inc(v.tensor_tensor(out=otm[:], in0=otm[:], in1=gtile[:], op=TT.add))
            vv.wait(v)
            vv.inc(v.bn_stats(out=stats[:, 0:6], in_=otm[:, 0:512]))
            vv.inc(v.bn_stats(out=stats[:, 6:12], in_=otm[:, 512:1024]))
            vv.wait(v)
            vv.inc(v.bn_aggr(out=aggr[:], in_=stats[:]))
        on(blk.vector, v_post)

        def s_ln(sc):
            vv.wait(sc)
            ss.inc(sc.activation(out=lnt[:, 0:1], in_=aggr[:, 1:2], func=AF.Ln,
                                 bias=epsc[:, 0:1]))
            ss.wait(sc)
            ss.inc(sc.activation(out=lnt[:, 1:2], in_=lnt[:, 0:1], func=AF.Exp,
                                 scale=-0.5))
        on(blk.scalar, s_ln)

        def v_x3(v):
            ss.wait(v)
            vv.inc(v.tensor_scalar(out=x3t[:], in0=otm[:], scalar1=aggr[:, 0:1],
                                   scalar2=lnt[:, 1:2], op0=TT.subtract, op1=TT.mult))
            vv.wait(v)
            vv.inc(v.tensor_tensor(out=x3t[:], in0=x3t[:], in1=gbc[:], op=TT.mult))
            vv.wait(v)
            vv.inc(v.tensor_tensor(out=x3t[:], in0=x3t[:], in1=bbc[:], op=TT.add))
            vv.wait(v)
            vv.inc(v.tensor_tensor(out=x3t[:], in0=x3t[:], in1=xt[:], op=TT.add))
            vv.wait(v)
            vv.inc(v.bn_stats(out=stats[:, 0:6], in_=x3t[:, 0:512]))
            vv.inc(v.bn_stats(out=stats[:, 6:12], in_=x3t[:, 512:1024]))
            vv.wait(v)
            vv.inc(v.bn_aggr(out=aggr[:], in_=stats[:]))
        on(blk.vector, v_x3)

        def s_ln2(sc):
            vv.wait(sc)
            ss.inc(sc.activation(out=lnt[:, 0:1], in_=aggr[:, 1:2], func=AF.Ln,
                                 bias=epsc[:, 0:1]))
            ss.wait(sc)
            ss.inc(sc.activation(out=lnt[:, 1:2], in_=lnt[:, 0:1], func=AF.Exp,
                                 scale=-0.5))
        on(blk.scalar, s_ln2)

        def v_xn3(v):
            ss.wait(v)
            vv.inc(v.tensor_scalar(out=xn3t[:], in0=x3t[:], scalar1=aggr[:, 0:1],
                                   scalar2=lnt[:, 1:2], op0=TT.subtract, op1=TT.mult))
            vv.wait(v)
            vv.inc(v.tensor_copy(out=xn3b[:], in_=xn3t[:]))
        on(blk.vector, v_xn3)

        for c in range(8):
            def t_tr2(te, c=c):
                ss.wait(te)
                if c == 0:
                    vv.wait(te)
                mm.inc(te.transpose(pfm[:], xn3b[:, c * 128:(c + 1) * 128], eyesb[:]))
            on(blk.tensor, t_tr2)

            def s_ev2(sc, c=c):
                mm.wait(sc)
                ss.inc(sc.copy(out=xnfm[:, c * 128:(c + 1) * 128], in_=pfm[:]))
            on(blk.scalar, s_ev2)

        def t_r1(te):
            ss.wait(te)
            for c in range(8):
                ins = te.matmul(prt[:, 0:RH], xnfm[:, c * 128:(c + 1) * 128],
                                rw1s[:, c * RH:(c + 1) * RH],
                                start=(c == 0), stop=(c == 7))
                if c == 7:
                    mm.inc(ins)
        on(blk.tensor, t_r1)

        def s_sil(sc):
            mm.wait(sc)
            ss.inc(sc.activation(out=sil[:, 0:RH], in_=prt[:, 0:RH], func=AF.Silu))
        on(blk.scalar, s_sil)

        def t_tr3(te):
            ss.wait(te)
            mm.inc(te.transpose(pfm[:], sil[:, 0:128], eyesb[:]))
        on(blk.tensor, t_tr3)

        def s_ev3(sc):
            mm.wait(sc)
            ss.inc(sc.copy(out=xnfm[:, 0:128], in_=pfm[:]))
        on(blk.scalar, s_ev3)

        def t_r2(te):
            ss.wait(te)
            mm.inc(te.matmul(prt[:, 0:NEXP], xnfm[0:RH, 0:128], rw2s[0:RH, :],
                             start=True, stop=True))
        on(blk.tensor, t_r2)

        def v_lg(v):
            mm.wait(v)
            vv.inc(v.tensor_copy(out=lgt[:], in_=prt[:, 0:NEXP]))
        on(blk.vector, v_lg)

        def s_st(s, r0=r0):
            vv.wait(s)
            st.inc(s.dma_start(out=x3_o[r0:r0 + 128, :], in_=x3t[:]), 16)
            st.inc(s.dma_start(out=xn3_o[r0:r0 + 128, :], in_=xn3t[:]), 16)
            st.inc(s.dma_start(out=lg_o[r0:r0 + 128, :], in_=lgt[:]), 16)
            st.wait(s)
        on(blk.sync, s_st)

    return L.done()



def build_moe_topk():
    L = LP("topk")
    nc = L.nc
    lgd = L.inp("lg", [8, S], F32)
    idx_o = L.out("idx", [8, CAP], U32)
    wts_o = L.out("wts", [8, CAP], F32)

    lgs = L.sb("lgs", [8, S], F32)
    nmx = L.sb("nmx", [8, 1], F32)
    rs = L.sb("rs", [8, 2], F32)
    mx8 = L.sb("mx8", [8, 8], F32)
    mi8 = L.sb("mi8", [8, 8], U32)
    vals = L.sb("vals", [8, CAP], F32)
    idxs = L.sb("idxs", [8, CAP], U32)
    wsum = L.sb("wsum", [8, 2], F32)

    ld = L.ctr("ld")
    vv = L.ctr("vv")
    ss = L.ctr("ss")
    st = L.ctr("st")
    blk = L.begin()

    def on(engf, f):
        engf(f)

    on(blk.sync, lambda s: (s.dma_start(out=lgs[:], in_=lgd[:]).then_inc(ld.sem, 16),
                            setattr(ld, "v", ld.v + 16))[-1])

    def v_sm(v):
        ld.wait(v)
        vv.inc(v.tensor_reduce(out=nmx[:], in_=lgs[:], axis=mybir.AxisListType.X,
                               op=TT.max, negate=True))
    on(blk.vector, v_sm)

    def s_exp(sc):
        vv.wait(sc)
        ss.inc(sc.activation(out=lgs[:], in_=lgs[:], func=AF.Exp, bias=nmx[:, 0:1],
                             accum_out=rs[:, 0:1]))
    on(blk.scalar, s_exp)

    def v_norm(v):
        ss.wait(v)
        vv.inc(v.reciprocal(out=rs[:, 1:2], in_=rs[:, 0:1]))
        vv.wait(v)
        vv.inc(v.tensor_scalar(out=lgs[:], in0=lgs[:], scalar1=rs[:, 1:2],
                               scalar2=None, op0=TT.mult))
        vv.wait(v)
        for r in range(CAP // 8):
            vv.inc(v.max(out=mx8[:], in_=lgs[:]))
            vv.wait(v)
            vv.inc(v.max_index(out=mi8[:], in_max=mx8[:], in_values=lgs[:]))
            vv.wait(v)
            vv.inc(v.match_replace(out=lgs[:], in_to_replace=mx8[:], in_values=lgs[:],
                                   imm_value=-1.0))
            vv.wait(v)
            vv.inc(v.tensor_copy(out=vals[:, r * 8:(r + 1) * 8], in_=mx8[:]))
            vv.inc(v.tensor_copy(out=idxs[:, r * 8:(r + 1) * 8], in_=mi8[:]))
            vv.wait(v)
        vv.inc(v.tensor_reduce(out=wsum[:, 0:1], in_=vals[:], axis=mybir.AxisListType.X,
                               op=TT.add))
        vv.wait(v)
        vv.inc(v.tensor_scalar(out=wsum[:, 0:1], in0=wsum[:, 0:1], scalar1=1e-9,
                               scalar2=None, op0=TT.add))
        vv.wait(v)
        vv.inc(v.reciprocal(out=wsum[:, 1:2], in_=wsum[:, 0:1]))
        vv.wait(v)
        vv.inc(v.tensor_scalar(out=vals[:], in0=vals[:], scalar1=wsum[:, 1:2],
                               scalar2=None, op0=TT.mult))
    on(blk.vector, v_norm)

    def s_st(s):
        vv.wait(s)
        st.inc(s.dma_start(out=idx_o[:], in_=idxs[:]), 16)
        st.inc(s.dma_start(out=wts_o[:], in_=vals[:]), 16)
        st.wait(s)
    on(blk.sync, s_st)

    return L.done()



def build_moe_expert():
    L = LP("moe")
    nc = L.nc
    h3 = L.inp("h3", [S, D], F32)
    idxd = L.inp("idx", [8, CAP], U32)
    wtsc = L.inp("wtsc", [128, 16], F32)
    w1d = L.inp("w1", [8, D, DFF], F32)
    w2d = L.inp("w2", [8, D, DFF], F32)
    w3d = L.inp("w3", [8, DFF, D], F32)
    eye = L.inp("eye", [128, 128], BF16)
    p_o = L.out("partial", [S, D], F32)

    w1s = L.sb("w1s", [128, 8 * DFF], BF16)
    w2s = L.sb("w2s", [128, 8 * DFF], BF16)
    w3s = L.sb("w3s", [128, 16 * D], BF16)
    eyesb = L.sb("eyesb", [128, 128], BF16)
    idxs = L.sb("idxs", [128, 2], U32)
    wtss = L.sb("wtss", [128, 16], F32)
    xg = L.sb("xg", [128, 2 * D], BF16)
    xgfm = L.sb("xgfm", [128, 8 * 256], BF16)
    gfm = L.sb("gfm", [128, 16 * 256], BF16)
    vfm = L.sb("vfm", [128, 16 * 256], BF16)
    zt = L.sb("zt", [128, D], F32)
    outs = L.sb("outs", [128, D], F32)
    pfm = L.ps("pfm", [128, 128], BF16)
    pp = L.ps("pp", [128, 256])
    pW = L.ps("pW", [128, 1024])

    ld = L.ctr("ld")
    lw = L.ctr("lw")
    lgx = L.ctr("lgx")
    mm = L.ctr("mm")
    vv = L.ctr("vv")
    ss = L.ctr("ss")
    sc2 = L.ctr("sc2")
    st = L.ctr("st")
    blk = L.begin()

    def on(engf, f):
        engf(f)

    def g_pre(g):
        ld.inc(g.dma_start(out=eyesb[:], in_=eye[:]), 16)
        ld.inc(g.dma_start(out=wtss[:], in_=wtsc[:]), 16)
    on(blk.gpsimd, g_pre)
    on(blk.vector, lambda v: vv.inc(v.memset(zt[:], 0.0)))

    def s_zero(s):
        vv.wait(s)
        for i in range(S // 128):
            st.inc(s.dma_start(out=p_o[i * 128:(i + 1) * 128, :], in_=zt[:]), 16)
        st.wait(s)
    on(blk.sync, s_zero)

    for e in range(8):
        def g_we(g, e=e):
            mm.wait(g)
            _load_weight_chunks(g, lw, w1s, w1d[e], D, DFF)
            _load_weight_chunks(g, lw, w2s, w2d[e], D, DFF)
            _load_weight_chunks(g, lw, w3s, w3d[e], DFF, D)
            for tch in range(2):
                lw.inc(g.dma_start(out=idxs[:, tch:tch + 1],
                                   in_=idxd[e:e + 1, tch * 128:(tch + 1) * 128]
                                   .rearrange("a (p o) -> (a p) o", o=1)), 16)
        on(blk.gpsimd, g_we)

        def g_gx(g, e=e):
            ss.wait(g)
            g.wait_ge(lw.sem, lw.v)
            for tch in range(2):
                g.indirect_dma_start(
                    out=xg[:, tch * D:(tch + 1) * D],
                    out_offset=None,
                    in_=h3[:],
                    in_offset=IndirectOffsetOnAxis(ap=idxs[:, tch:tch + 1], axis=0),
                ).then_inc(lgx.sem, 16)
                lgx.v += 16
        on(blk.gpsimd, g_gx)

        for tch in range(2):
            for c in range(8):
                def t_tr(te, tch=tch, c=c):
                    ss.wait(te)
                    if tch == 0 and c == 0:
                        lgx.wait(te)
                    mm.inc(te.transpose(pfm[:], xg[:, tch * D + c * 128:tch * D + (c + 1) * 128],
                                        eyesb[:]))
                on(blk.tensor, t_tr)

                def s_ev(sc, tch=tch, c=c):
                    mm.wait(sc)
                    ss.inc(sc.copy(out=xgfm[:, c * 256 + tch * 128:c * 256 + tch * 128 + 128],
                                   in_=pfm[:]))
                on(blk.scalar, s_ev)

        for which, wss, dst, fn in ((0, w1s, gfm, AF.Silu), (1, w2s, vfm, AF.Identity)):
            for oc in range(16):
                def t_mm(te, wss=wss, oc=oc):
                    ss.wait(te)
                    if oc == 0:
                        vv.wait(te)
                    for c in range(8):
                        ins = te.matmul(pp[:], wss[:, c * DFF + oc * 128:c * DFF + oc * 128 + 128],
                                        xgfm[:, c * 256:(c + 1) * 256],
                                        start=(c == 0), stop=(c == 7))
                        if c == 7:
                            mm.inc(ins)
                on(blk.tensor, t_mm)

                def s_ev2(sc, dst=dst, fn=fn, oc=oc):
                    mm.wait(sc)
                    ss.inc(sc.activation(out=dst[:, oc * 256:(oc + 1) * 256], in_=pp[:], func=fn))
                on(blk.scalar, s_ev2)

        def v_gv(v):
            ss.wait(v)
            for oc in range(16):
                vv.inc(v.tensor_tensor(out=gfm[:, oc * 256:(oc + 1) * 256],
                                       in0=gfm[:, oc * 256:(oc + 1) * 256],
                                       in1=vfm[:, oc * 256:(oc + 1) * 256], op=TT.mult))
        on(blk.vector, v_gv)

        for tch in range(2):
            def t_o(te, tch=tch, e=e):
                vv.wait(te)
                for c in range(16):
                    ins = te.matmul(pW[:, 0:512],
                                    gfm[:, c * 256 + tch * 128:c * 256 + tch * 128 + 128],
                                    w3s[:, c * D:c * D + 512],
                                    start=(c == 0), stop=(c == 15))
                    if c == 15:
                        mm.inc(ins)
                for c in range(16):
                    ins = te.matmul(pW[:, 512:1024],
                                    gfm[:, c * 256 + tch * 128:c * 256 + tch * 128 + 128],
                                    w3s[:, c * D + 512:c * D + 1024],
                                    start=(c == 0), stop=(c == 15))
                    if c == 15:
                        mm.inc(ins)
            on(blk.tensor, t_o)

            def v_w(v, e=e, tch=tch):
                mm.wait(v)
                v.wait_ge(sc2.sem, sc2.v)
                vv.inc(v.tensor_scalar(out=outs[:], in0=pW[:], scalar1=wtss[:, 2 * e + tch:2 * e + tch + 1],
                                       scalar2=None, op0=TT.mult))
            on(blk.vector, v_w)

            def g_sc(g, e=e, tch=tch):
                vv.wait(g)
                if e == 0 and tch == 0:
                    g.wait_ge(st.sem, st.v)
                g.wait_ge(sc2.sem, sc2.v)
                g.indirect_dma_start(
                    out=p_o[:],
                    out_offset=IndirectOffsetOnAxis(ap=idxs[:, tch:tch + 1], axis=0),
                    in_=outs[:],
                    in_offset=None,
                    compute_op=TT.add,
                ).then_inc(sc2.sem, 16)
                sc2.v += 16
                g.wait_ge(sc2.sem, sc2.v)
            on(blk.gpsimd, g_sc)

    def s_fin(s):
        s.wait_ge(sc2.sem, sc2.v)
        st.wait(s)
    on(blk.sync, s_fin)

    return L.done()



def build_combine():
    L = LP("comb")
    nc = L.nc
    x3 = L.inp("x3", [TPC, D], F32)
    pa = L.inp("pa", [TPC, D], F32)
    pb = L.inp("pb", [TPC, D], F32)
    o = L.out("out", [TPC, D], F32)
    a = L.sb("a", [128, D], F32)
    b2 = L.sb("b2", [128, D], F32)
    c2 = L.sb("c2", [128, D], F32)
    ld = L.ctr("ld")
    vv = L.ctr("vv")
    st = L.ctr("st")
    blk = L.begin()

    def on(engf, f):
        engf(f)

    for t in range(TPC // 128):
        r0 = t * 128

        def s_ld(s, r0=r0, t=t):
            if t > 0:
                st.wait(s)
            s.dma_start(out=a[:], in_=x3[r0:r0 + 128, :]).then_inc(ld.sem, 16)
            s.dma_start(out=b2[:], in_=pa[r0:r0 + 128, :]).then_inc(ld.sem, 16)
            s.dma_start(out=c2[:], in_=pb[r0:r0 + 128, :]).then_inc(ld.sem, 16)
            ld.v += 48
        on(blk.sync, s_ld)

        def v_add(v):
            v.wait_ge(ld.sem, ld.v)
            vv.inc(v.tensor_tensor(out=a[:], in0=a[:], in1=b2[:], op=TT.add))
            vv.wait(v)
            vv.inc(v.tensor_tensor(out=a[:], in0=a[:], in1=c2[:], op=TT.add))
        on(blk.vector, v_add)

        def s_st(s, r0=r0):
            vv.wait(s)
            st.inc(s.dma_start(out=o[r0:r0 + 128, :], in_=a[:]), 16)
            st.wait(s)
        on(blk.sync, s_st)

    return L.done()



def l17_inmaps(x2_flat, W):
    ims = []
    for r in range(NCORES):
        ims.append({
            "x": f32c(x2_flat[r * TPC:(r + 1) * TPC]),
            "eye": _EYE,
            "wq": f32c(W["L1_attn_Wq"]),
            "wk": f32c(W["L1_attn_Wk"]),
            "wv": f32c(W["L1_attn_Wv"]),
        })
    return ims


def build_wt_host(rel_table):
    rt = np.asarray(rel_table, dtype=np.float32)
    ext = np.concatenate([rt, np.broadcast_to(rt[2046:2047], (256, 16))], 0)
    extf = np.ascontiguousarray(ext).reshape(-1)
    j = np.arange(S)[:, None]
    u = np.arange(2048)[None, :]
    idx = 16 * np.minimum(u, 2046) + j
    return extf[idx]


def _u_indices():
    ii = np.arange(S)
    h = np.arange(16)
    u = np.minimum(256 * (ii[:, None] % 16) - ii[:, None] // 16 + 4095 - 256 * h[None, :], 2046)
    return (u * 16 + h[None, :]).astype(np.uint32)


def l18_inmaps(WT, vf):
    ims = []
    for r in range(NCORES):
        b = r // 2
        uh = r % 2
        ims.append({
            "wt": f32c(WT[:, uh * 1024:(uh + 1) * 1024]),
            "v": np.ascontiguousarray(vf[b * S:(b + 1) * S]),
        })
    return ims


def l19_inmaps(qf, kf, vf, WVfull, x2_flat, W):
    uix = _u_indices()
    ones = np.ones((1, 64), dtype=BF)
    ims = []
    for r in range(NCORES):
        b = r // 2
        s0 = (r % 2) * TPC
        rows = slice(b * S, (b + 1) * S)
        qa = np.zeros((8, 128, TPC), dtype=BF)
        ka = np.zeros((8, 128, S), dtype=BF)
        va = np.zeros((S, 1040), dtype=BF)
        for hp in range(8):
            for a in range(2):
                hh = hp * 2 + a
                qa[hp, a * 64:(a + 1) * 64] = qf[rows, 64 * hh:64 * hh + 64][s0:s0 + TPC].T
                ka[hp, a * 64:(a + 1) * 64] = kf[rows, 64 * hh:64 * hh + 64].T
        for hh in range(16):
            va[:, hh * 65:hh * 65 + 64] = vf[rows, 64 * hh:64 * hh + 64]
            va[:, hh * 65 + 64] = np.ones(S, dtype=BF)
        ims.append({
            "q": qa, "k": ka, "v": va,
            "wvd": f32c(WVfull[b]),
            "uix": np.ascontiguousarray(uix[s0:s0 + TPC]),
            "x2": f32c(x2_flat[r * TPC:(r + 1) * TPC]),
            "gmr": f32c(W["L1_attn_gamma"]).reshape(1, D),
            "btr": f32c(W["L1_attn_beta"]).reshape(1, D),
            "rw1": f32c(W["L1_ffn_router_w1"]),
            "rw2": f32c(W["L1_ffn_router_w2"]),
            "eye": _EYE,
            "ones1": ones,
        })
    return ims


def l21_inmaps(lg_flat):
    ims = []
    for r in range(NCORES):
        b = r // 2
        eh = r % 2
        lgb = lg_flat[b * S:(b + 1) * S]
        ims.append({"lg": np.ascontiguousarray(lgb[:, eh * 8:(eh + 1) * 8].T)})
    return ims


def l22_inmaps(xn3_flat, idx_res, wts_res, W):
    ims = []
    for r in range(NCORES):
        b = r // 2
        eh = r % 2
        es = slice(eh * 8, (eh + 1) * 8)
        wts = wts_res[r]
        wtsc = np.zeros((128, 16), np.float32)
        for e in range(8):
            for tch in range(2):
                wtsc[:, 2 * e + tch] = wts[e, tch * 128:(tch + 1) * 128]
        ims.append({
            "h3": f32c(xn3_flat.reshape(B, S, D)[b]),
            "idx": np.ascontiguousarray(idx_res[r]),
            "wtsc": wtsc,
            "w1": f32c(W["L1_ffn_w1"][es]),
            "w2": f32c(W["L1_ffn_w2"][es]),
            "w3": f32c(W["L1_ffn_w3"][es]),
            "eye": _EYE,
        })
    return ims


def l23_inmaps(x3_flat, partials):
    ims = []
    for r in range(NCORES):
        b = r // 2
        s0 = (r % 2) * TPC
        ims.append({
            "x3": f32c(x3_flat[r * TPC:(r + 1) * TPC]),
            "pa": f32c(partials[2 * b][s0:s0 + TPC]),
            "pb": f32c(partials[2 * b + 1][s0:s0 + TPC]),
        })
    return ims
